# version 06
# speedup vs baseline: 1.0720x; 1.0093x over previous
import numpy as np
import ml_dtypes
from contextlib import ExitStack

import concourse.bass as bass
import concourse.mybir as mybir
from concourse.bass_utils import run_bass_kernel_spmd

F32 = mybir.dt.float32
BF16 = mybir.dt.bfloat16
I32 = mybir.dt.int32
AF = mybir.ActivationFunctionType
ALU = mybir.AluOpType
AX = mybir.AxisListType
NPBF16 = ml_dtypes.bfloat16

NCORES = 8
B, S, D = 2, 4096, 4096
T = B * S
TPC = T // NCORES
NTT = TPC // 128
EPS = 1e-5

ENGINES = ("tensor", "vector", "scalar", "gpsimd", "sync")


class Prog:

    def __init__(self, name="p"):
        self.nc = bass.Bass("TRN2", target_bir_lowering=False)
        self.name = name
        self.steps = {e: [] for e in ENGINES}
        self.cnt = {e: 0 for e in ENGINES}
        self.dma_cnt = {}
        self.dma_engine = {}
        self.es = ExitStack()
        self._finalized = False

    def din(self, name, shape, dtype):
        return self.nc.dram_tensor(name, list(shape), dtype, kind="ExternalInput").ap()

    def dout(self, name, shape, dtype):
        return self.nc.dram_tensor(name, list(shape), dtype, kind="ExternalOutput").ap()

    def dtmp(self, name, shape, dtype):
        return self.nc.dram_tensor(name, list(shape), dtype, kind="Internal").ap()

    def sbuf(self, name, shape, dtype):
        cm = self.nc.sbuf_tensor(name, list(shape), dtype)
        h = cm.__enter__()
        if not hasattr(self, "_scoped"):
            self._scoped = {}
        self._scoped[name] = cm
        self.es.callback(lambda n=name: self.sfree(n))
        return h

    def sfree(self, *names):
        for n in names:
            cm = self._scoped.pop(n, None)
            if cm is not None:
                cm.__exit__(None, None, None)

    def psum(self, name, shape, dtype):
        cm = self.nc.psum_tensor(name, list(shape), dtype)
        h = cm.__enter__()
        if not hasattr(self, "_scoped"):
            self._scoped = {}
        self._scoped[name] = cm
        self.es.callback(lambda n=name: self.sfree(n))
        return h

    def _norm(self, dep):
        e, k = dep
        if e == "gpsimd":
            return (e, self.cnt["gpsimd"])
        if e.startswith("D:"):
            return (e, self.dma_cnt[e[2:]])
        return dep

    def op(self, e, fn, deps=()):
        deps = tuple(self._norm(d) for d in deps)
        self.steps[e].append(("op", fn, deps))
        self.cnt[e] += 1
        return (e, self.cnt[e])

    def dma(self, e, stream, fn, deps=()):
        assert self.dma_engine.setdefault(stream, e) == e, (
            f"stream {stream} used from both {self.dma_engine[stream]} and {e}")
        deps = tuple(self._norm(d) for d in deps)
        self.steps[e].append(("dma", stream, fn, deps))
        self.dma_cnt[stream] = self.dma_cnt.get(stream, 0) + 1
        return ("D:" + stream, self.dma_cnt[stream])

    def build(self):
        assert not self._finalized
        self._finalized = True
        nc = self.nc
        sems = {}
        for e in ENGINES:
            if self.cnt[e]:
                sems[e] = self.es.enter_context(nc.semaphore("sem_" + e))
        for s in self.dma_cnt:
            sems["D:" + s] = self.es.enter_context(nc.semaphore("sem_D_" + s))

        block = self.es.enter_context(nc.Block())

        def make_body(e):
            def body(eng):
                waited = {}
                for step in self.steps[e]:
                    if step[0] == "op":
                        _, fn, deps = step
                        inc_key, inc_val = e, 1
                    else:
                        _, stream, fn, deps = step
                        inc_key, inc_val = "D:" + stream, 16
                    want = {}
                    for (dk, dv) in deps:
                        v = dv * 16 if dk.startswith("D:") else dv
                        if want.get(dk, 0) < v:
                            want[dk] = v
                    for dk, v in want.items():
                        if waited.get(dk, 0) < v:
                            eng.wait_ge(sems[dk], v)
                            waited[dk] = v
                    inst = fn(eng)
                    inst.then_inc(sems[inc_key], inc_val)
                for s, cnt in self.dma_cnt.items():
                    if self.dma_engine[s] == e:
                        v = cnt * 16
                        if waited.get("D:" + s, 0) < v:
                            eng.wait_ge(sems["D:" + s], v)
            return body

        for e in ENGINES:
            if self.steps[e]:
                getattr(block, e)(make_body(e))
        self.es.close()
        return nc



def emit_const_setup(p):
    ident = p.sbuf("c_ident", [128, 128], BF16)
    ones = p.sbuf("c_ones", [1, 128], BF16)
    eps_c = p.sbuf("c_eps", [128, 1], F32)
    t1 = p.op("gpsimd", lambda e: e.memset(ident[:, :], 0.0))
    t2 = p.op("gpsimd", lambda e: e.memset(ones[0:1, :], 1.0))
    te = p.op("gpsimd", lambda e: e.memset(eps_c[:, :], EPS))
    t3 = p.op("gpsimd", lambda e: e.affine_select(
        out=ident[:, :], in_=ident[:, :], compare_op=ALU.not_equal, fill=1.0,
        base=0, pattern=[[-1, 128]], channel_multiplier=1), deps=[t1])
    return ident, ones, eps_c, t3, t2, te


def emit_ln_stats(p, x_ap, sq_out, mean, var, rstd, nmr, width, deps):
    inv = 1.0 / width
    t_sum = p.op("vector", lambda e: e.reduce_sum(mean[:, 0:1], x_ap, axis=AX.X), deps=deps)
    t_mean = p.op("vector", lambda e: e.tensor_scalar_mul(mean[:, 0:1], mean[:, 0:1], inv), deps=[t_sum])
    t_sq = p.op("scalar", lambda e: e.activation(sq_out, x_ap, AF.Square,
                                                 accum_out=var[:, 0:1]), deps=deps)
    t_msq = p.op("vector", lambda e: e.tensor_tensor(nmr[:, 0:1], mean[:, 0:1], mean[:, 0:1], op=ALU.mult),
                 deps=[t_mean])
    t_var = p.op("vector", lambda e: e.scalar_tensor_tensor(
        var[:, 0:1], var[:, 0:1], inv, nmr[:, 0:1], op0=ALU.mult, op1=ALU.subtract),
        deps=[t_sq, t_msq])
    t_ln = p.op("scalar", lambda e: e.activation(rstd[:, 0:1], var[:, 0:1], AF.Ln,
                                                 bias=p.eps_c[:, 0:1]), deps=[t_var, p.t_eps])
    t_ex = p.op("scalar", lambda e: e.activation(rstd[:, 0:1], rstd[:, 0:1], AF.Exp, scale=-0.5), deps=[t_ln])
    t_nmr = p.op("vector", lambda e: e.scalar_tensor_tensor(
        nmr[:, 0:1], mean[:, 0:1], -1.0, rstd[:, 0:1], op0=ALU.mult, op1=ALU.mult),
        deps=[t_ex, t_msq])
    return t_ex, t_nmr, t_sq



def build_launch_A():
    p = Prog("A")
    x_d = p.din("x", [TPC, D], F32)
    w_d = [p.din(n, [D, D], F32) for n in ("wqT", "wkT", "wvT")]
    b_d = [p.din(n, [D], F32) for n in ("bq", "bk", "bv")]
    out_d = [p.dout(n, [TPC, D], BF16) for n in ("q", "k", "v")]

    ident, ones, eps_c, t_ident, t_ones, t_eps = emit_const_setup(p)
    p.eps_c, p.t_eps = eps_c, t_eps

    xin = [p.sbuf(f"xin{i}", [128, D], F32) for i in range(2)]
    z = [p.sbuf(f"z{i}", [128, D], BF16) for i in range(2)]
    zT = p.sbuf("zT", [128, NTT * D], BF16)
    wt = [p.sbuf(f"wt{i}", [128, 32 * 512], BF16) for i in range(2)]
    stg = [p.sbuf(f"stg{i}", [128, 512], BF16) for i in range(2)]
    bias_bf = [p.sbuf(f"bias{i}", [1, D], BF16) for i in range(3)]
    stat = [[p.sbuf(f"st{n}{i}", [128, 1], F32) for n in ("m", "v", "r", "n")] for i in range(2)]

    ps_tr = [p.psum(f"ps_tr{i}", [128, 1024], BF16) for i in range(2)]
    ps_mm = [p.psum(f"ps_mm{i}", [128, 512], F32) for i in range(4)]

    t_bias = []
    for i in range(3):
        t_bias.append(p.dma("gpsimd", "bias", (lambda i: lambda e: e.dma_start(
            out=bias_bf[i][0:1, :], in_=b_d[i].unsqueeze(0)))(i)))

    t_zT_done = []
    xin_free = [(), ()]
    stat_free = [(), ()]
    z_free = [(), ()]
    trps_free = [(), ()]
    for t in range(NTT):
        pb = t % 2
        t_x = p.dma("sync", f"xin{pb}", (lambda t, pb: lambda e: e.dma_start(
            out=xin[pb][:, :], in_=x_d[t * 128:(t + 1) * 128, :]))(t, pb),
            deps=xin_free[pb])
        m_, v_, r_, n_ = stat[pb]
        t_r, t_n, t_sq = emit_ln_stats(p, xin[pb][:, :], z[pb][:, :], m_, v_, r_, n_, D,
                                       deps=[t_x] + list(stat_free[pb]) + list(z_free[pb]))
        t_z = p.op("scalar", (lambda pb, r_, n_: lambda e: e.activation(
            z[pb][:, :], xin[pb][:, :], AF.Identity, bias=n_[:, 0:1], scale=r_[:, 0:1]))(pb, r_, n_),
            deps=[t_r, t_n, t_x, t_sq])
        stat_free[pb] = (t_z, t_n)
        xin_free[pb] = (t_z,)
        evac_toks = []
        for grp in range(4):
            tp = grp % 2
            tr_toks = []
            for j in range(8):
                kc = grp * 8 + j
                tok = p.op("tensor", (lambda pb, tp, j, kc: lambda e: e.transpose(
                    ps_tr[tp][:, j * 128:(j + 1) * 128],
                    z[pb][:, kc * 128:(kc + 1) * 128], ident[:, :]))(pb, tp, j, kc),
                    deps=[t_z, t_ident] + list(trps_free[tp]))
                tr_toks.append(tok)
            t_evac = p.op("vector", (lambda t, grp, tp: lambda e: e.tensor_copy(
                zT[:, t * D + grp * 1024: t * D + (grp + 1) * 1024], ps_tr[tp][:, :]))(t, grp, tp),
                deps=tr_toks)
            trps_free[tp] = (t_evac,)
            evac_toks.append(t_evac)
        z_free[pb] = (evac_toks[-1],)
        t_zT_done.append(tuple(evac_toks))

    zT_ready = tuple(tok for toks in t_zT_done for tok in toks)

    wt_free = [(), ()]
    stg_free = [(), ()]
    bank_free = [()] * 4
    wi = 0
    for w in range(3):
        for ob in range(8):
            wb = wi % 2
            wi += 1
            wtoks = []
            for part in range(4):
                kc0, kc1 = part * 8, (part + 1) * 8
                wtoks.append(p.dma("gpsimd", f"wload{wb}", (lambda w, ob, wb, kc0, kc1: lambda e: e.dma_start(
                    out=wt[wb].rearrange("p (kc o) -> p kc o", kc=32)[:, kc0:kc1, :],
                    in_=w_d[w].rearrange("(kc p) o -> p kc o", p=128)[:, kc0:kc1, ob * 512:(ob + 1) * 512]))(w, ob, wb, kc0, kc1),
                    deps=wt_free[wb]))
            wt_reads = []
            for t in range(NTT):
                bk = (t + (wi - 1) * NTT) % 4
                mm_toks = []
                for kc in range(32):
                    tok = p.op("tensor", (lambda t, kc, bk, wb: lambda e: e.matmul(
                        ps_mm[bk][:, :],
                        zT[:, t * D + kc * 128:t * D + (kc + 1) * 128],
                        wt[wb][:, kc * 512:(kc + 1) * 512],
                        start=(kc == 0), stop=False))(t, kc, bk, wb),
                        deps=(list(wtoks) + list(zT_ready) + list(bank_free[bk])) if kc == 0 else [])
                    mm_toks.append(tok)
                t_bmm = p.op("tensor", (lambda w, ob, bk: lambda e: e.matmul(
                    ps_mm[bk][:, :], ones[0:1, :], bias_bf[w][0:1, ob * 512:(ob + 1) * 512],
                    start=False, stop=True))(w, ob, bk),
                    deps=[t_bias[w], t_ones, mm_toks[-1]])
                sg = (t + (wi - 1) * NTT) % 2
                t_evac = p.op("vector", (lambda sg, bk: lambda e: e.tensor_copy(
                    stg[sg][:, :], ps_mm[bk][:, :]))(sg, bk),
                    deps=[t_bmm] + list(stg_free[sg]))
                bank_free[bk] = (t_evac,)
                t_st = p.dma("scalar", f"qout{sg}", (lambda w, ob, t, sg: lambda e: e.dma_start(
                    out=out_d[w][t * 128:(t + 1) * 128, ob * 512:(ob + 1) * 512],
                    in_=stg[sg][:, :]))(w, ob, t, sg),
                    deps=[t_evac])
                stg_free[sg] = (t_st,)
                wt_reads.append(mm_toks[-1])
                wt_reads.append(t_bmm)
            wt_free[wb] = tuple(wt_reads[-2:]) + (wt_reads[0],)
    return p



NPAIR = 8
ISQD = 1.0 / np.sqrt(128.0)


def build_launch_B():
    p = Prog("B")
    qhd_d = p.din("qhd", [128, 32, TPC], BF16)
    khd_d = p.din("khd", [128, 32, TPC], BF16)
    vh_d = p.din("vh", [32, TPC, 128], BF16)
    ctx_d = p.dout("ctxh", [32, TPC, 128], BF16)

    ident, ones, eps_c, t_ident, t_ones, t_eps = emit_const_setup(p)
    qt = [p.sbuf(f"qt{i}", [128, 32 * 128], BF16) for i in range(2)]
    kt = [p.sbuf(f"kt{i}", [128, 32 * 128], BF16) for i in range(2)]
    vt = [p.sbuf(f"vt{i}", [32, 128 * 128], BF16) for i in range(2)]
    pb_sb = p.sbuf("pb_sb", [32, 512], BF16)
    pT_sb = p.sbuf("pT_sb", [32, 512], BF16)
    zs = p.sbuf("zs", [32, 16], F32)
    ctx_sb = [p.sbuf(f"ctx_sb{i}", [32, 2048], BF16) for i in range(2)]

    ps_sc = [p.psum(f"ps_sc{i}", [32, 512], F32) for i in range(2)]
    ps_tp = p.psum("ps_tp", [32, 512], BF16)
    ps_cx = [p.psum(f"ps_cx{i}", [32, 512], F32) for i in range(2)]

    ISQ = 1.0 / np.sqrt(128.0)
    in_free = [(), ()]
    sc_free = [(), ()]
    tp_free = ()
    cx_free = [(), ()]
    pb_free = ()
    pT_free = ()
    zs_free = ()
    ctxsb_free = [(), ()]

    for t in range(NTT):
        ib = t % 2
        t_lq = p.dma("sync", f"ld{ib}", (lambda t, ib: lambda e: e.dma_start(
            out=qt[ib][:, :].rearrange("p (h w) -> p h w", h=32),
            in_=qhd_d[:, :, t * 128:(t + 1) * 128]))(t, ib), deps=in_free[ib])
        t_lk = p.dma("sync", f"ld{ib}", (lambda t, ib: lambda e: e.dma_start(
            out=kt[ib][:, :].rearrange("p (h w) -> p h w", h=32),
            in_=khd_d[:, :, t * 128:(t + 1) * 128]))(t, ib), deps=in_free[ib])
        t_lv = p.dma("sync", f"ld{ib}", (lambda t, ib: lambda e: e.dma_start(
            out=vt[ib][:, :].rearrange("p (w d) -> p w d", w=128),
            in_=vh_d[:, t * 128:(t + 1) * 128, :]))(t, ib), deps=in_free[ib])
        t_ld = (t_lq, t_lk, t_lv)
        reads = []
        for b16 in range(8):
            sp = b16 % 2
            smms = []
            for j in range(16):
                w = b16 * 16 + j
                smms.append(p.op("tensor", (lambda ib, sp, j, w: lambda e: e.matmul(
                    ps_sc[sp][:, j * 32:(j + 1) * 32],
                    qt[ib][:, :].rearrange("p (h w) -> p h w", h=32)[:, :, w],
                    kt[ib][:, :].rearrange("p (h w) -> p h w", h=32)[:, :, w],
                    start=True, stop=True))(ib, sp, j, w),
                    deps=(list(t_ld) + list(sc_free[sp])) if j == 0 else []))
            t_exp = p.op("scalar", (lambda sp: lambda e: e.activation(
                pb_sb[0:32, :], ps_sc[sp][:, :], AF.Exp, scale=ISQ))(sp),
                deps=smms + list(pb_free))
            sc_free[sp] = (t_exp,)
            t_z = p.op("vector", lambda e: e.tensor_reduce(
                zs[0:32, 0:16], pb_sb[0:32, :].rearrange("p (w t2) -> p w t2", w=16),
                axis=AX.X, op=ALU.add), deps=[t_exp] + list(zs_free))
            t_rz = p.op("vector", lambda e: e.reciprocal(zs[0:32, 0:16], zs[0:32, 0:16]), deps=[t_z])
            trs = []
            for j in range(16):
                trs.append(p.op("tensor", (lambda j: lambda e: e.transpose(
                    ps_tp[0:32, j * 32:(j + 1) * 32], pb_sb[0:32, j * 32:(j + 1) * 32],
                    ident[0:32, 0:32]))(j),
                    deps=([t_exp, t_ident] + list(tp_free)) if j == 0 else [t_exp]))
            t_pt = p.op("vector", lambda e: e.tensor_copy(pT_sb[0:32, :], ps_tp[0:32, :]),
                        deps=trs + list(pT_free))
            tp_free = (t_pt,)
            pb_free = (trs[-1], t_z)
            for g4 in range(4):
                cb = g4 % 2
                cmms = []
                for j4 in range(4):
                    j = g4 * 4 + j4
                    w = b16 * 16 + j
                    cmms.append(p.op("tensor", (lambda ib, cb, j4, j, w: lambda e: e.matmul(
                        ps_cx[cb][:, j4 * 128:(j4 + 1) * 128],
                        pT_sb[0:32, j * 32:(j + 1) * 32],
                        vt[ib][:, :].rearrange("p (w d) -> p w d", w=128)[:, w, :],
                        start=True, stop=True))(ib, cb, j4, j, w),
                        deps=([t_pt] + list(cx_free[cb])) if j4 == 0 else []))
                t_ce = p.op("vector", (lambda sp2, cb, b16, g4: lambda e: e.tensor_tensor(
                    ctx_sb[sp2][:, g4 * 512:(g4 + 1) * 512].rearrange("p (w d) -> p w d", w=4),
                    ps_cx[cb][:, :].rearrange("p (w d) -> p w d", w=4),
                    zs[0:32, g4 * 4:(g4 + 1) * 4].unsqueeze(2).broadcast_to([32, 4, 128]),
                    op=ALU.mult))(b16 % 2, cb, b16, g4),
                    deps=cmms + [t_rz] + list(ctxsb_free[b16 % 2]))
                cx_free[cb] = (t_ce,)
                reads.append(cmms[-1])
                if g4 == 3:
                    zs_free = (t_ce,)
                    t_out = p.dma("scalar", f"out{b16 % 2}", (lambda t, b16, sp2: lambda e: e.dma_start(
                        out=ctx_d[:, t * 128 + b16 * 16:t * 128 + (b16 + 1) * 16, :],
                        in_=ctx_sb[sp2][:, :].rearrange("p (w d) -> p w d", w=16)))(t, b16, b16 % 2),
                        deps=[t_ce])
                    ctxsb_free[b16 % 2] = (t_out,)
            pT_free = (reads[-1],)
        in_free[ib] = (reads[-1], t_exp)
    return p


def run_B(q, k, v):
    p = build_launch_B()
    nc = p.build()
    in_maps = []
    for c in range(NCORES):
        sl = slice(c * TPC, (c + 1) * TPC)
        qs = np.asarray(q[sl]).reshape(TPC, 32, 128)
        ks = np.asarray(k[sl]).reshape(TPC, 32, 128)
        vs = np.asarray(v[sl]).reshape(TPC, 32, 128)
        in_maps.append({
            "qhd": np.ascontiguousarray(qs.transpose(2, 1, 0)),
            "khd": np.ascontiguousarray(ks.transpose(2, 1, 0)),
            "vh": np.ascontiguousarray(vs.transpose(1, 0, 2)),
        })
    res = run_bass_kernel_spmd(nc, in_maps, core_ids=list(range(NCORES)))
    ctx = np.zeros((T, D), dtype=NPBF16)
    for c in range(NCORES):
        sl = slice(c * TPC, (c + 1) * TPC)
        ctx[sl] = res.results[c]["ctxh"].transpose(1, 0, 2).reshape(TPC, D)
    return ctx



def build_launch_C():
    p = Prog("C")
    x_d = p.din("x", [TPC, D], F32)
    ctx_d = p.din("ctx", [TPC, D], BF16)
    bwT_d = p.din("baseWT", [D, D], F32)
    rwT_d = p.din("routerWT", [D, 15], F32)
    a2_d = p.din("A2", [D, 128], F32)
    b2_d = p.din("B2", [128, D], F32)
    bb_d = p.din("base_b", [D], F32)
    rb_d = p.din("router_b", [15], F32)
    em_d = p.din("em", [15], I32)
    x2_d = p.dout("x2", [TPC, D], F32)
    x1_s = p.dtmp("x1_s", [TPC, D], F32)
    fo_s = p.dtmp("fo_s", [TPC, D], F32)

    ident, ones, eps_c, t_ident, t_ones, t_eps = emit_const_setup(p)
    p.eps_c, p.t_eps = eps_c, t_eps

    zT = p.sbuf("zT", [128, NTT * D], BF16)
    stat = [[p.sbuf(f"st{n}{i}", [128, 1], F32) for n in ("m", "v", "r", "n")] for i in range(2)]
    ps_tr = [p.psum(f"ps_tr{i}", [128, 1024], BF16) for i in range(2)]
    xin = [p.sbuf(f"xin{i}", [128, D], F32) for i in range(2)]
    cin = [p.sbuf(f"cin{i}", [128, D], BF16) for i in range(2)]
    ctxn = p.sbuf("ctxn", [128, D], F32)
    z = [p.sbuf(f"z{i}", [128, D], BF16) for i in range(2)]

    xin_free = [(), ()]
    cin_free = [(), ()]
    ctxn_free = ()
    stat_free = [(), ()]
    z_free = [(), ()]
    trps_free = [(), ()]
    phaseA_end = []
    for t in range(NTT):
        pb = t % 2
        t_x = p.dma("sync", f"xin{pb}", (lambda t, pb: lambda e: e.dma_start(
            out=xin[pb][:, :], in_=x_d[t * 128:(t + 1) * 128, :]))(t, pb),
            deps=xin_free[pb])
        t_c = p.dma("sync", f"cin{pb}", (lambda t, pb: lambda e: e.dma_start(
            out=cin[pb][:, :], in_=ctx_d[t * 128:(t + 1) * 128, :]))(t, pb),
            deps=cin_free[pb])
        m_, v_, r_, n_ = stat[pb]
        t_r, t_n, t_sq = emit_ln_stats(p, cin[pb][:, :], z[pb][:, :], m_, v_, r_, n_, D,
                                       deps=[t_c] + list(stat_free[pb]) + list(z_free[pb]))
        t_cn = p.op("scalar", (lambda pb, r_, n_: lambda e: e.activation(
            ctxn[:, :], cin[pb][:, :], AF.Identity, bias=n_[:, 0:1], scale=r_[:, 0:1]))(pb, r_, n_),
            deps=[t_r, t_n, t_sq] + list(ctxn_free))
        cin_free[pb] = (t_cn,)
        t_x1 = p.op("vector", (lambda pb: lambda e: e.tensor_tensor(
            xin[pb][:, :], xin[pb][:, :], ctxn[:, :], op=ALU.add))(pb),
            deps=[t_x, t_cn])
        ctxn_free = (t_x1,)
        t_sx1 = p.dma("scalar", f"x1o{pb}", (lambda t, pb: lambda e: e.dma_start(
            out=x1_s[t * 128:(t + 1) * 128, :], in_=xin[pb][:, :]))(t, pb), deps=[t_x1])
        t_r2, t_n2, t_sq2 = emit_ln_stats(p, xin[pb][:, :], z[pb][:, :], m_, v_, r_, n_, D,
                                          deps=[t_x1, t_r, t_n])
        t_z = p.op("scalar", (lambda pb, r_, n_: lambda e: e.activation(
            z[pb][:, :], xin[pb][:, :], AF.Identity, bias=n_[:, 0:1], scale=r_[:, 0:1]))(pb, r_, n_),
            deps=[t_r2, t_n2, t_sq2])
        stat_free[pb] = (t_z, t_n2)
        xin_free[pb] = (t_z, t_sx1)
        evac_toks = []
        for grp in range(4):
            tp = grp % 2
            tr_toks = []
            for j in range(8):
                kc = grp * 8 + j
                tr_toks.append(p.op("tensor", (lambda pb, tp, j, kc: lambda e: e.transpose(
                    ps_tr[tp][:, j * 128:(j + 1) * 128],
                    z[pb][:, kc * 128:(kc + 1) * 128], ident[:, :]))(pb, tp, j, kc),
                    deps=[t_z, t_ident] + list(trps_free[tp])))
            t_evac = p.op("vector", (lambda t, grp, tp: lambda e: e.tensor_copy(
                zT[:, t * D + grp * 1024: t * D + (grp + 1) * 1024], ps_tr[tp][:, :]))(t, grp, tp),
                deps=tr_toks)
            trps_free[tp] = (t_evac,)
            evac_toks.append(t_evac)
        z_free[pb] = (evac_toks[-1],)
        phaseA_end.extend(evac_toks)
    phaseA_end.extend(xin_free[0] + xin_free[1] + cin_free[0] + cin_free[1] + ctxn_free)
    phaseA_end = tuple(phaseA_end)
    p.sfree("z1", "z0", "ctxn", "cin1", "cin0", "xin1", "xin0")

    a2 = p.sbuf("a2", [128, 32 * 128], BF16)
    b2 = p.sbuf("b2", [128, D], BF16)
    rw = p.sbuf("rw", [128, 32 * 15], BF16)
    bias_bf = p.sbuf("bias_bf", [1, D], BF16)
    rb_bf = p.sbuf("rb_bf", [1, 15], BF16)
    em_i = p.sbuf("em_i", [15, 1], I32)
    em_f = p.sbuf("em_f", [15, 1], F32)
    Mb = p.sbuf("Mb", [15, 8], BF16)

    t_a2 = p.dma("gpsimd", "smallw", lambda e: e.dma_start(
        out=a2[:, :].rearrange("p (kc r) -> p kc r", kc=32),
        in_=a2_d.rearrange("(kc p) r -> p kc r", p=128)), deps=phaseA_end)
    t_b2 = p.dma("gpsimd", "smallw", lambda e: e.dma_start(out=b2[:, :], in_=b2_d[:, :]),
                 deps=phaseA_end)
    t_rw = p.dma("gpsimd", "smallw", lambda e: e.dma_start(
        out=rw[:, :].rearrange("p (kc r) -> p kc r", kc=32),
        in_=rwT_d.rearrange("(kc p) r -> p kc r", p=128)), deps=phaseA_end)
    t_bb = p.dma("gpsimd", "smallw", lambda e: e.dma_start(
        out=bias_bf[0:1, :], in_=bb_d.unsqueeze(0)), deps=phaseA_end)
    t_rb = p.dma("gpsimd", "smallw", lambda e: e.dma_start(
        out=rb_bf[0:1, :], in_=rb_d.unsqueeze(0)), deps=phaseA_end)
    t_em = p.dma("gpsimd", "smallw", lambda e: e.dma_start(
        out=em_i[:, 0:1], in_=em_d.unsqueeze(1)), deps=phaseA_end)
    t_smallw = (t_a2, t_b2, t_rw, t_bb, t_rb, t_em)
    t_emf = p.op("vector", lambda e: e.tensor_copy(em_f[:, 0:1], em_i[:, 0:1]), deps=[t_em])
    t_M = []
    for ex in range(8):
        t_M.append(p.op("vector", (lambda ex: lambda e: e.tensor_single_scalar(
            Mb[:, ex:ex + 1], em_f[:, 0:1], float(ex), op=ALU.is_equal))(ex), deps=[t_emf]))

    pr_sc = p.sbuf("pr_sc", [128, 256], F32)
    d2 = p.sbuf("d2", [128, 240], F32)
    s15 = p.sbuf("s15", [128, 15], F32)
    ind15 = p.sbuf("ind15", [128, 15], F32)
    ptil = p.sbuf("ptil", [128, 16], BF16)
    ptT = p.sbuf("ptT", [15, 128], BF16)
    rstat = [p.sbuf(f"rstat{i}", [128, 1], F32) for i in range(3)]
    w8_all = p.sbuf("w8_all", [128, NTT * 8], F32)
    htil = p.sbuf("htil", [128, 128], BF16)
    hT_all = p.sbuf("hT_all", [128, NTT * 128], BF16)

    ps_aux = p.psum("ps_aux", [128, 128], F32)
    ps_tp2 = p.psum("ps_tp2", [128, 128], BF16)

    aux_free = ()
    tp2_free = ()
    pr_free = ()
    d2_free = ()
    s15_free = ()
    ind_free = ()
    ptil_free = ()
    ptT_free = ()
    rstat_free = ()
    htil_free = ()
    tile_prep_end = []
    for t in range(NTT):
        mms = []
        for kc in range(32):
            mms.append(p.op("tensor", (lambda t, kc: lambda e: e.matmul(
                ps_aux[:, 0:15], zT[:, t * D + kc * 128:t * D + (kc + 1) * 128],
                rw[:, kc * 15:(kc + 1) * 15], start=(kc == 0), stop=False))(t, kc),
                deps=(list(t_smallw) + list(aux_free) + list(phaseA_end)) if kc == 0 else ()))
        t_lb = p.op("tensor", lambda e: e.matmul(
            ps_aux[:, 0:15], ones[0:1, :], rb_bf[0:1, 0:15], start=False, stop=True),
            deps=[mms[-1], t_ones])
        t_mx = p.op("vector", lambda e: e.reduce_max(rstat[0][:, 0:1], ps_aux[:, 0:15], axis=AX.X),
                    deps=[t_lb] + list(rstat_free))
        t_nmx = p.op("vector", lambda e: e.tensor_scalar_mul(rstat[0][:, 0:1], rstat[0][:, 0:1], -1.0),
                     deps=[t_mx])
        t_exp = p.op("scalar", lambda e: e.activation(
            pr_sc[:, 0:15], ps_aux[:, 0:15], AF.Exp, bias=rstat[0][:, 0:1],
            accum_out=rstat[1][:, 0:1]), deps=[t_nmx] + list(pr_free))
        t_rz = p.op("vector", lambda e: e.reciprocal(rstat[2][:, 0:1], rstat[1][:, 0:1]), deps=[t_exp])
        t_pn = p.op("vector", lambda e: e.tensor_scalar_mul(
            pr_sc[:, 0:15], pr_sc[:, 0:15], rstat[2][:, 0:1]), deps=[t_rz])
        aux_free_router = (t_exp, t_mx)
        rstat_free = (t_pn,)
        t_d2 = p.op("vector", lambda e: e.tensor_tensor(
            d2[:, 0:225].rearrange("p (n m) -> p n m", n=15),
            pr_sc[:, 0:15].unsqueeze(1).broadcast_to([128, 15, 15]),
            pr_sc[:, 0:15].unsqueeze(2).broadcast_to([128, 15, 15]),
            op=ALU.subtract), deps=[t_pn] + list(d2_free))
        t_sg = p.op("scalar", lambda e: e.activation(d2[:, 0:225], d2[:, 0:225], AF.Sign), deps=[t_d2])
        t_s15 = p.op("vector", lambda e: e.tensor_reduce(
            s15[:, 0:15], d2[:, 0:225].rearrange("p (n m) -> p n m", n=15),
            axis=AX.X, op=ALU.add), deps=[t_sg] + list(s15_free))
        d2_free = (t_s15,)
        t_ind = p.op("vector", lambda e: e.tensor_single_scalar(
            ind15[:, 0:15], s15[:, 0:15], 0.5, op=ALU.is_lt), deps=[t_s15] + list(ind_free))
        s15_free = (t_ind,)
        t_pt = p.op("vector", lambda e: e.scalar_tensor_tensor(
            ptil[:, 0:15], pr_sc[:, 0:15], 2.0, ind15[:, 0:15], op0=ALU.mult, op1=ALU.mult),
            deps=[t_ind] + list(ptil_free))
        ind_free = (t_pt,)
        pr_free = (t_pt, t_d2)
        t_tp = p.op("tensor", lambda e: e.transpose(
            ps_tp2[0:15, 0:128], ptil[:, 0:15], ident[:, :]),
            deps=[t_pt, t_ident] + list(tp2_free))
        t_ptev = p.op("vector", lambda e: e.tensor_copy(ptT[0:15, 0:128], ps_tp2[0:15, 0:128]),
                      deps=[t_tp] + list(ptT_free))
        tp2_free = (t_ptev,)
        ptil_free = (t_tp,)
        t_w8 = p.op("tensor", (lambda t: lambda e: e.matmul(
            ps_aux[:, 16:24], ptT[0:15, 0:128], Mb[0:15, 0:8], start=True, stop=True))(t),
            deps=[t_ptev] + list(t_M) + list(aux_free_router))
        t_w8e = p.op("vector", (lambda t: lambda e: e.tensor_copy(
            w8_all[:, t * 8:(t + 1) * 8], ps_aux[:, 16:24]))(t), deps=[t_w8])
        ptT_free = (t_w8,)
        hmms = []
        for kc in range(32):
            hmms.append(p.op("tensor", (lambda t, kc: lambda e: e.matmul(
                ps_aux[:, 0:128],
                zT[:, t * D + kc * 128:t * D + (kc + 1) * 128],
                a2[:, kc * 128:(kc + 1) * 128], start=(kc == 0), stop=(kc == 31)))(t, kc),
                deps=[t_w8e, t_exp, t_mx] if kc == 0 else ()))
        t_ht = p.op("vector", (lambda t: lambda e: e.tensor_tensor(
            htil[:, :].rearrange("p (ex r) -> p ex r", ex=8),
            ps_aux[:, 0:128].rearrange("p (ex r) -> p ex r", ex=8),
            w8_all[:, t * 8:(t + 1) * 8].unsqueeze(2).broadcast_to([128, 8, 16]),
            op=ALU.mult))(t), deps=[hmms[-1], t_w8e] + list(htil_free))
        aux_free = (t_ht,)
        t_htT = p.op("tensor", lambda e: e.transpose(
            ps_tp2[:, 0:128], htil[:, :], ident[:, :]), deps=[t_ht, t_ident] + list(tp2_free))
        htil_free = (t_htT,)
        t_htev = p.op("vector", (lambda t: lambda e: e.tensor_copy(
            hT_all[:, t * 128:(t + 1) * 128], ps_tp2[:, 0:128]))(t), deps=[t_htT])
        tp2_free = (t_htev,)
        tile_prep_end.append(t_htev)
    tile_prep_end = tuple(tile_prep_end)

    stg = [p.sbuf(f"stg{i}", [128, 512], F32) for i in range(2)]
    wt = [p.sbuf(f"wt{i}", [128, 32 * 512], BF16) for i in range(2)]
    ps_mm = [p.psum(f"ps_mm{i}", [128, 512], F32) for i in range(4)]

    wt_free = [tile_prep_end, tile_prep_end]
    stg_free = [(), ()]
    bank_free = [()] * 4
    last_fo = []
    for ob in range(8):
        wb = ob % 2
        wtoks = []
        for part in range(4):
            kc0, kc1 = part * 8, (part + 1) * 8
            wtoks.append(p.dma("gpsimd", f"wload{wb}", (lambda ob, wb, kc0, kc1: lambda e: e.dma_start(
                out=wt[wb].rearrange("p (kc o) -> p kc o", kc=32)[:, kc0:kc1, :],
                in_=bwT_d.rearrange("(kc p) o -> p kc o", p=128)[:, kc0:kc1, ob * 512:(ob + 1) * 512]))(ob, wb, kc0, kc1),
                deps=wt_free[wb]))
        wt_reads = []
        for t in range(NTT):
            bk = (t + ob * NTT) % 4
            mm_toks = []
            for kc in range(32):
                mm_toks.append(p.op("tensor", (lambda t, kc, bk, wb: lambda e: e.matmul(
                    ps_mm[bk][:, :],
                    zT[:, t * D + kc * 128:t * D + (kc + 1) * 128],
                    wt[wb][:, kc * 512:(kc + 1) * 512],
                    start=(kc == 0), stop=False))(t, kc, bk, wb),
                    deps=(list(wtoks) + list(bank_free[bk]) + list(tile_prep_end)) if kc == 0 else []))
            t_lora = p.op("tensor", (lambda t, ob, bk: lambda e: e.matmul(
                ps_mm[bk][:, :], hT_all[:, t * 128:(t + 1) * 128],
                b2[:, ob * 512:(ob + 1) * 512], start=False, stop=False))(t, ob, bk),
                deps=[mm_toks[-1]])
            t_bmm = p.op("tensor", (lambda ob, bk: lambda e: e.matmul(
                ps_mm[bk][:, :], ones[0:1, :], bias_bf[0:1, ob * 512:(ob + 1) * 512],
                start=False, stop=True))(ob, bk), deps=[t_lora, t_ones])
            sg = (t + ob * NTT) % 2
            t_evac = p.op("vector", (lambda sg, bk: lambda e: e.tensor_copy(
                stg[sg][:, :], ps_mm[bk][:, :]))(sg, bk), deps=[t_bmm] + list(stg_free[sg]))
            bank_free[bk] = (t_evac,)
            t_st = p.dma("scalar", f"foout{sg}", (lambda ob, t, sg: lambda e: e.dma_start(
                out=fo_s[t * 128:(t + 1) * 128, ob * 512:(ob + 1) * 512],
                in_=stg[sg][:, :]))(ob, t, sg), deps=[t_evac])
            stg_free[sg] = (t_st,)
            wt_reads.append(t_bmm)
        wt_free[wb] = (wt_reads[-1],)
        last_fo.append(("D:foout0", p.dma_cnt.get("foout0", 0)))
        last_fo.append(("D:foout1", p.dma_cnt.get("foout1", 0)))
    p.sfree("wt1", "wt0")

    fin = [p.sbuf(f"fin{i}", [128, D], F32) for i in range(2)]
    x1b = [p.sbuf(f"x1b{i}", [128, D], F32) for i in range(2)]
    fsq = p.sbuf("fsq", [128, D], BF16)
    fo_done = tuple(last_fo[-2:])
    fin_free = [(), ()]
    x1b_free = [(), ()]
    fsq_free = ()
    cstat_free = [stat_free[0], stat_free[1]]
    for t in range(NTT):
        pb = t % 2
        t_f = p.dma("sync", f"fin{pb}", (lambda t, pb: lambda e: e.dma_start(
            out=fin[pb][:, :], in_=fo_s[t * 128:(t + 1) * 128, :]))(t, pb),
            deps=list(fo_done) + list(fin_free[pb]))
        t_x1l = p.dma("sync", f"x1in{pb}", (lambda t, pb: lambda e: e.dma_start(
            out=x1b[pb][:, :], in_=x1_s[t * 128:(t + 1) * 128, :]))(t, pb),
            deps=x1b_free[pb])
        m_, v_, r_, n_ = stat[pb]
        t_r, t_n, t_sq = emit_ln_stats(p, fin[pb][:, :], fsq[:, :], m_, v_, r_, n_, D,
                                       deps=[t_f] + list(cstat_free[pb]) + list(fsq_free))
        fsq_free = (t_sq,)
        t_fn = p.op("scalar", (lambda pb, r_, n_: lambda e: e.activation(
            fin[pb][:, :], fin[pb][:, :], AF.Identity, bias=n_[:, 0:1], scale=r_[:, 0:1]))(pb, r_, n_),
            deps=[t_r, t_n, t_sq])
        t_x2 = p.op("vector", (lambda pb: lambda e: e.tensor_tensor(
            x1b[pb][:, :], x1b[pb][:, :], fin[pb][:, :], op=ALU.add))(pb),
            deps=[t_fn, t_x1l])
        cstat_free[pb] = (t_fn, t_n)
        fin_free[pb] = (t_x2,)
        t_o = p.dma("scalar", f"x2o{pb}", (lambda t, pb: lambda e: e.dma_start(
            out=x2_d[t * 128:(t + 1) * 128, :], in_=x1b[pb][:, :]))(t, pb), deps=[t_x2])
        x1b_free[pb] = (t_o,)
    return p


def run_C(x_tok, ctx, base_W, base_b, router_W, router_b, A, Bm, expert_map):
    p = build_launch_C()
    nc = p.build()
    baseWT = np.ascontiguousarray(base_W.T)
    routerWT = np.ascontiguousarray(router_W.T)
    A2 = np.ascontiguousarray(A.reshape(128, D).T)
    B2 = np.ascontiguousarray(Bm.transpose(0, 2, 1).reshape(128, D))
    in_maps = []
    for c in range(NCORES):
        sl = slice(c * TPC, (c + 1) * TPC)
        in_maps.append({
            "x": np.ascontiguousarray(x_tok[sl]), "ctx": np.ascontiguousarray(ctx[sl]),
            "baseWT": baseWT, "routerWT": routerWT, "A2": A2, "B2": B2,
            "base_b": base_b, "router_b": router_b, "em": expert_map.astype(np.int32),
        })
    res = run_bass_kernel_spmd(nc, in_maps, core_ids=list(range(NCORES)))
    return np.concatenate([r["x2"] for r in res.results], axis=0)



PROJ = 18560


def build_launch_D():
    p = Prog("D")
    x_d = p.din("x2", [TPC, D], F32)
    w_d = p.din("wipT", [D, PROJ], F32)
    out_d = p.dout("pj", [TPC, PROJ], BF16)

    ident, ones, eps_c, t_ident, t_ones, t_eps = emit_const_setup(p)
    p.eps_c, p.t_eps = eps_c, t_eps

    zT = p.sbuf("zT", [128, NTT * D], BF16)
    stat = [[p.sbuf(f"st{n}{i}", [128, 1], F32) for n in ("m", "v", "r", "n")] for i in range(2)]
    ps_tr = [p.psum(f"ps_tr{i}", [128, 1024], BF16) for i in range(2)]
    xin = [p.sbuf(f"xin{i}", [128, D], F32) for i in range(2)]
    z = [p.sbuf(f"z{i}", [128, D], BF16) for i in range(2)]

    xin_free = [(), ()]
    stat_free = [(), ()]
    z_free = [(), ()]
    trps_free = [(), ()]
    phaseA_end = []
    for t in range(NTT):
        pb = t % 2
        t_x = p.dma("sync", f"xin{pb}", (lambda t, pb: lambda e: e.dma_start(
            out=xin[pb][:, :], in_=x_d[t * 128:(t + 1) * 128, :]))(t, pb),
            deps=xin_free[pb])
        m_, v_, r_, n_ = stat[pb]
        t_r, t_n, t_sq = emit_ln_stats(p, xin[pb][:, :], z[pb][:, :], m_, v_, r_, n_, D,
                                       deps=[t_x] + list(stat_free[pb]) + list(z_free[pb]))
        t_z = p.op("scalar", (lambda pb, r_, n_: lambda e: e.activation(
            z[pb][:, :], xin[pb][:, :], AF.Identity, bias=n_[:, 0:1], scale=r_[:, 0:1]))(pb, r_, n_),
            deps=[t_r, t_n, t_sq])
        stat_free[pb] = (t_z, t_n)
        xin_free[pb] = (t_z,)
        evac_toks = []
        for grp in range(4):
            tp = grp % 2
            tr_toks = []
            for j in range(8):
                kc = grp * 8 + j
                tr_toks.append(p.op("tensor", (lambda pb, tp, j, kc: lambda e: e.transpose(
                    ps_tr[tp][:, j * 128:(j + 1) * 128],
                    z[pb][:, kc * 128:(kc + 1) * 128], ident[:, :]))(pb, tp, j, kc),
                    deps=[t_z, t_ident] + list(trps_free[tp])))
            t_evac = p.op("vector", (lambda t, grp, tp: lambda e: e.tensor_copy(
                zT[:, t * D + grp * 1024: t * D + (grp + 1) * 1024], ps_tr[tp][:, :]))(t, grp, tp),
                deps=tr_toks)
            trps_free[tp] = (t_evac,)
            evac_toks.append(t_evac)
        z_free[pb] = (evac_toks[-1],)
        phaseA_end.extend(evac_toks)
    phaseA_end = tuple(phaseA_end) + xin_free[0] + xin_free[1]
    p.sfree("z1", "z0", "xin1", "xin0")

    stg = [p.sbuf(f"stg{i}", [128, 512], BF16) for i in range(2)]
    wt = [p.sbuf(f"wt{i}", [128, 32 * 512], BF16) for i in range(2)]
    ps_mm = [p.psum(f"ps_mm{i}", [128, 512], F32) for i in range(4)]

    obs = [(i * 512, 512) for i in range(36)] + [(36 * 512, 128)]
    wt_free = [phaseA_end, phaseA_end]
    stg_free = [(), ()]
    bank_free = [()] * 4
    it = 0
    for (o0, ow) in obs:
        wb = it % 2
        it += 1
        wtoks = []
        for part in range(4):
            kc0, kc1 = part * 8, (part + 1) * 8
            wtoks.append(p.dma("gpsimd", f"wload{wb}", (lambda o0, ow, wb, kc0, kc1: lambda e: e.dma_start(
                out=wt[wb].rearrange("p (kc o) -> p kc o", kc=32)[:, kc0:kc1, 0:ow],
                in_=w_d.rearrange("(kc p) o -> p kc o", p=128)[:, kc0:kc1, o0:o0 + ow]))(o0, ow, wb, kc0, kc1),
                deps=wt_free[wb]))
        wt_reads = []
        for t in range(NTT):
            bk = (t + (it - 1) * NTT) % 4
            mm_toks = []
            for kc in range(32):
                mm_toks.append(p.op("tensor", (lambda t, kc, bk, wb, ow: lambda e: e.matmul(
                    ps_mm[bk][:, 0:ow],
                    zT[:, t * D + kc * 128:t * D + (kc + 1) * 128],
                    wt[wb][:, kc * 512:kc * 512 + ow],
                    start=(kc == 0), stop=(kc == 31)))(t, kc, bk, wb, ow),
                    deps=(list(wtoks) + list(bank_free[bk]) + list(phaseA_end)) if kc == 0 else []))
            sg = (t + (it - 1) * NTT) % 2
            t_evac = p.op("vector", (lambda sg, bk, ow: lambda e: e.tensor_copy(
                stg[sg][:, 0:ow], ps_mm[bk][:, 0:ow]))(sg, bk, ow), deps=[mm_toks[-1]] + list(stg_free[sg]))
            bank_free[bk] = (t_evac,)
            t_st = p.dma("scalar", f"pout{sg}", (lambda o0, ow, t, sg: lambda e: e.dma_start(
                out=out_d[t * 128:(t + 1) * 128, o0:o0 + ow],
                in_=stg[sg][:, 0:ow]))(o0, ow, t, sg), deps=[t_evac])
            stg_free[sg] = (t_st,)
            wt_reads.append(mm_toks[-1])
        wt_free[wb] = (wt_reads[-1],)
    return p


def run_D(x2, in_proj_w):
    p = build_launch_D()
    nc = p.build()
    wipT = np.ascontiguousarray(in_proj_w.T)
    in_maps = []
    for c in range(NCORES):
        sl = slice(c * TPC, (c + 1) * TPC)
        in_maps.append({"x2": np.ascontiguousarray(x2[sl]), "wipT": wipT})
    res = run_bass_kernel_spmd(nc, in_maps, core_ids=list(range(NCORES)))
    return np.concatenate([r["pj"] for r in res.results], axis=0)



NCH = 32 * 64
NCK = 20
SS = 128
HD = 64


def build_launch_E():
    p = Prog("E")
    xhT_d = p.din("xhT", [NCH, S + 3], BF16)
    bcT_d = p.din("bcT", [512, S + 3], BF16)
    dtc_d = p.din("dtc", [S, 32], BF16)
    cw_d = p.din("cw", [128, NCK * 4], F32)
    cb_d = p.din("cb", [128, NCK], F32)
    dtb_d = p.din("dtb", [32], F32)
    Dh_d = p.din("Dh", [32], F32)
    arow_d = p.din("arow", [32], F32)
    y_d = p.dout("y", [S, NCH], BF16)
    xtm_s = p.dtmp("xtm_s", [S, NCH], BF16)

    ident, ones, eps_c, t_ident, t_ones, t_eps = emit_const_setup(p)
    p.eps_c, p.t_eps = eps_c, t_eps
    onesf = p.sbuf("c_onesf", [1, 128], F32)
    triu = p.sbuf("c_triu", [128, 128], F32)
    t_onesf = p.op("gpsimd", lambda e: e.memset(onesf[0:1, :], 1.0))
    t_tr0 = p.op("gpsimd", lambda e: e.memset(triu[:, :], 0.0))
    t_triu = p.op("gpsimd", lambda e: e.affine_select(
        out=triu[:, :], in_=triu[:, :], compare_op=ALU.is_gt, fill=1.0,
        base=0, pattern=[[-1, 128]], channel_multiplier=1), deps=[t_tr0])
    onec = p.sbuf("c_onec", [128, 1], F32)
    t_onec = p.op("gpsimd", lambda e: e.memset(onec[:, :], 1.0))
    identf = p.sbuf("c_identf", [128, 128], F32)
    t_if0 = p.op("gpsimd", lambda e: e.memset(identf[:, :], 0.0))
    t_identf = p.op("gpsimd", lambda e: e.affine_select(
        out=identf[:, :], in_=identf[:, :], compare_op=ALU.not_equal, fill=1.0,
        base=0, pattern=[[-1, 128]], channel_multiplier=1), deps=[t_if0])

    cw_sb = p.sbuf("cw_sb", [128, NCK * 4], F32)
    cb_sb = p.sbuf("cb_sb", [128, NCK], F32)
    bcf = p.sbuf("bcf", [128, 4 * S], BF16)
    bct = p.sbuf("bct", [128, 32 * 256], BF16)
    cin = [p.sbuf(f"cin{i}", [128, S + 3], BF16) for i in range(2)]
    diag_t = p.sbuf("diag_t", [128, 4 * 128], BF16)
    cin1 = [p.sbuf(f"cin1{i}", [128, S + 2], BF16) for i in range(2)]
    cout = [p.sbuf(f"cout{i}", [128, S], BF16) for i in range(2)]
    xstg = [p.sbuf(f"xstg{i}", [128, 1024], BF16) for i in range(2)]
    ps_tr = [p.psum(f"ps_tr{i}", [128, 1024], BF16) for i in range(2)]
    ps_cv = [p.psum(f"ps_cv{i}", [128, 512], F32) for i in range(2)]

    t_cw = p.dma("gpsimd", "cwl", lambda e: e.dma_start(out=cw_sb[:, :], in_=cw_d[:, :]))
    t_cb = p.dma("gpsimd", "cwl", lambda e: e.dma_start(out=cb_sb[:, :], in_=cb_d[:, :]))

    cin_free = [(), ()]
    cin1_free = [(), ()]
    diag_free = ()
    pscv_free = [(), ()]
    cout_free = [(), ()]
    xstg_free = [(), ()]
    trps_free = [(), ()]
    bct_toks = []
    bcf_toks = []
    xtm_last = []
    for ck in range(NCK):
        pb = ck % 2
        if ck < 16:
            t_ci = p.dma("sync", f"cin{pb}", (lambda ck, pb: lambda e: e.dma_start(
                out=cin[pb][:, :], in_=xhT_d[ck * 128:(ck + 1) * 128, :]))(ck, pb),
                deps=cin_free[pb])
        else:
            t_ci = p.dma("sync", f"cin{pb}", (lambda ck, pb: lambda e: e.dma_start(
                out=cin[pb][:, :], in_=bcT_d[(ck - 16) * 128:(ck - 15) * 128, :]))(ck, pb),
                deps=cin_free[pb])
        t_ci1 = p.dma("sync", f"cin1{pb}", (lambda ck, pb: lambda e: e.dma_start(
            out=cin1[pb][:, :],
            in_=(xhT_d[ck * 128:(ck + 1) * 128, 1:] if ck < 16
                 else bcT_d[(ck - 16) * 128:(ck - 15) * 128, 1:])))(ck, pb),
            deps=cin1_free[pb])
        cwc = lambda ck, j: cw_sb[:, ck * 4 + j:ck * 4 + j + 1]
        dg_toks = []
        for j in range(4):
            dg_toks.append(p.op("vector", (lambda ck, j: lambda e: e.tensor_scalar_mul(
                diag_t[:, j * 128:(j + 1) * 128], ident[:, :], cwc(ck, j)))(ck, j),
                deps=[t_cw, t_ident] + list(diag_free)))
        cv_toks = []
        for cb in range(8):
            cvb = cb % 2
            mms = []
            for j in range(4):
                srcbuf = cin[pb] if j % 2 == 0 else cin1[pb]
                off = (j // 2) * 2 + cb * 512
                mms.append(p.op("tensor", (lambda j, cvb, srcbuf, off: lambda e: e.matmul(
                    ps_cv[cvb][:, :], diag_t[:, j * 128:(j + 1) * 128],
                    srcbuf[:, off:off + 512],
                    start=(j == 0), stop=(j == 3)))(j, cvb, srcbuf, off),
                    deps=(dg_toks + [t_ci, t_ci1] + list(pscv_free[cvb])) if j == 0 else []))
            t_cv = p.op("scalar", (lambda ck, pb, cb, cvb: lambda e: e.activation(
                (cout[pb] if ck < 16 else bcf)[:, ((0 if ck < 16 else (ck - 16) * S) + cb * 512):][:, 0:512],
                ps_cv[cvb][:, :], AF.Silu, bias=cb_sb[:, ck:ck + 1]))(ck, pb, cb, cvb),
                deps=[mms[-1], t_cb] + (list(cout_free[pb]) if cb == 0 else []))
            pscv_free[cvb] = (t_cv,)
            cv_toks.append(t_cv)
        diag_free = tuple(cv_toks[-2:])
        cin_free[pb] = (cv_toks[-1],)
        cin1_free[pb] = (cv_toks[-1],)
        t_cv_all = cv_toks
        if ck >= 16:
            bcf_toks.extend(cv_toks)
        if ck < 16 or ck in (16, 17):
            for grp in range(4):
                tp = grp % 2
                trs = []
                for j in range(8):
                    tc = grp * 8 + j
                    trs.append(p.op("tensor", (lambda pb, tp, j, tc, ck: lambda e: e.transpose(
                        ps_tr[tp][:, j * 128:(j + 1) * 128],
                        (cout[pb] if ck < 16 else bcf)[:, (tc * 128 if ck < 16 else (ck - 16) * S + tc * 128):][:, 0:128],
                        ident[:, :]))(pb, tp, j, tc, ck),
                        deps=t_cv_all + [t_ident] + list(trps_free[tp])))
                if ck < 16:
                    sg = grp % 2
                    t_ev = p.op("vector", (lambda sg, tp: lambda e: e.tensor_copy(
                        xstg[sg][:, :], ps_tr[tp][:, :]))(sg, tp),
                        deps=trs + list(xstg_free[sg]))
                    trps_free[tp] = (t_ev,)
                    t_xo = p.dma("scalar", f"xtm{sg}", (lambda ck, grp, sg: lambda e: e.dma_start(
                        out=xtm_s.rearrange("(tc q) c -> q tc c", q=128)[:, grp * 8:(grp + 1) * 8, ck * 128:(ck + 1) * 128],
                        in_=xstg[sg][:, :].rearrange("q (tc c) -> q tc c", tc=8)))(ck, grp, sg),
                        deps=[t_ev])
                    xstg_free[sg] = (t_xo,)
                    xtm_last.append(t_xo)
                else:
                    g = ck - 16
                    t_ev = p.op("vector", (lambda g, grp, tp: lambda e: e.tensor_copy(
                        bct[:, :].rearrange("q (tc w) -> q tc w", tc=32)[:, grp * 8:(grp + 1) * 8, g * 128:(g + 1) * 128],
                        ps_tr[tp][:, :].rearrange("q (j c) -> q j c", j=8)))(g, grp, tp),
                        deps=trs)
                    trps_free[tp] = (t_ev,)
                    bct_toks.append(t_ev)
        cout_free[pb] = trps_free[0] + trps_free[1]
    bcf_ready = tuple(bcf_toks)
    bct_ready = tuple(bct_toks)
    xtm_done = tuple(xtm_last[-2:])
    conv_end = (tuple(trps_free[0]) + tuple(trps_free[1]) + tuple(xstg_free[0]) +
                tuple(xstg_free[1]) + tuple(diag_free) +
                tuple(cin_free[0]) + tuple(cin_free[1]) + tuple(cin1_free[0]) + tuple(cin1_free[1]) +
                tuple(pscv_free[0]) + tuple(pscv_free[1]))
    p.sfree("ps_cv1", "ps_cv0", "ps_tr1", "ps_tr0")
    p.sfree("xstg1", "xstg0", "cout1", "cout0", "cin11", "cin10", "diag_t", "cin1", "cin0")

    dtt = p.sbuf("dtt", [128, 32 * 32], F32)
    dtt_bf = p.sbuf("dtt_bf", [128, 32 * 32], BF16)
    dsc = p.sbuf("dsc", [128, 32 * 32], F32)
    ps_sm = p.psum("ps_sm", [128, 512], F32)
    psm = []
    t_dt = p.dma("gpsimd", "dtl", lambda e: e.dma_start(
        out=dtt[:, :].rearrange("q (tc h) -> q tc h", tc=32),
        in_=dtc_d.rearrange("(tc q) h -> q tc h", q=128)), deps=conv_end)
    dtb_r = p.sbuf("dtb_r", [1, 32], F32)
    arow_r = p.sbuf("arow_r", [1, 32], F32)
    Dh_r = p.sbuf("Dh_r", [1, 32], F32)
    t_dtb = p.dma("gpsimd", "dtl", lambda e: e.dma_start(out=dtb_r[0:1, :], in_=dtb_d.unsqueeze(0)), deps=conv_end)
    t_ar = p.dma("gpsimd", "dtl", lambda e: e.dma_start(out=arow_r[0:1, :], in_=arow_d.unsqueeze(0)), deps=conv_end)
    t_dh = p.dma("gpsimd", "dtl", lambda e: e.dma_start(out=Dh_r[0:1, :], in_=Dh_d.unsqueeze(0)), deps=conv_end)
    dtb_bc = p.sbuf("dtb_bc", [128, 32], F32)
    arow_bc = p.sbuf("arow_bc", [128, 32], F32)
    dxd_bc = p.sbuf("dxd_bc", [128, 2048], BF16)

    def psm_op(eng_name, fn, deps=()):
        tok = p.op(eng_name, fn, deps=list(deps) + psm[-1:])
        psm.append(tok)
        return tok

    t_b1 = psm_op("tensor", lambda e: e.matmul(ps_sm[:, 0:32], onesf[0:1, :], dtb_r[0:1, :],
                                               start=True, stop=True),
                  deps=[t_dtb, t_onesf] + list(conv_end))
    t_b1e = psm_op("vector", lambda e: e.tensor_copy(dtb_bc[:, :], ps_sm[:, 0:32]))
    t_b2 = psm_op("tensor", lambda e: e.matmul(ps_sm[:, 0:32], onesf[0:1, :], arow_r[0:1, :],
                                               start=True, stop=True), deps=[t_ar])
    t_b2e = psm_op("vector", lambda e: e.tensor_copy(arow_bc[:, :], ps_sm[:, 0:32]))
    for q4 in range(4):
        psm_op("tensor", (lambda q4: lambda e: e.matmul(
            ps_sm[:, 0:512], onesf[0:1, :],
            Dh_r[0:1, q4 * 8:(q4 + 1) * 8].unsqueeze(2).broadcast_to([1, 8, 64]),
            start=True, stop=True))(q4), deps=[t_dh])
        psm_op("vector", (lambda q4: lambda e: e.tensor_copy(
            dxd_bc[:, q4 * 512:(q4 + 1) * 512], ps_sm[:, 0:512]))(q4))
    t_dxd = psm[-1]
    t_s0 = p.op("vector", lambda e: e.tensor_tensor(
        dtt[:, :].rearrange("q (tc h) -> q tc h", tc=32),
        dtt[:, :].rearrange("q (tc h) -> q tc h", tc=32),
        dtb_bc[:, :].unsqueeze(1).broadcast_to([128, 32, 32]),
        op=ALU.add), deps=[t_dt, t_b1e])
    t_s1 = p.op("scalar", lambda e: e.activation(dsc[:, :], dtt[:, :], AF.Abs), deps=[t_s0])
    t_s2 = p.op("scalar", lambda e: e.activation(dsc[:, :], dsc[:, :], AF.Exp, scale=-1.0), deps=[t_s1])
    t_s3 = p.op("scalar", lambda e: e.activation(dsc[:, :], dsc[:, :], AF.Ln, bias=onec[:, 0:1]),
                deps=[t_s2, t_onec])
    t_s4 = p.op("vector", lambda e: e.tensor_single_scalar(dtt[:, :], dtt[:, :], 0.0, op=ALU.max),
                deps=[t_s0, t_s1])
    t_s5 = p.op("vector", lambda e: e.tensor_tensor(dtt[:, :], dtt[:, :], dsc[:, :], op=ALU.add),
                deps=[t_s4, t_s3])
    t_s6 = p.op("vector", lambda e: e.tensor_scalar(dtt[:, :], dtt[:, :], 1e-3, 100.0,
                                                    op0=ALU.max, op1=ALU.min), deps=[t_s5])
    t_s7 = p.op("vector", lambda e: e.tensor_copy(dtt_bf[:, :], dtt[:, :]), deps=[t_s6])
    t_dA = p.op("vector", lambda e: e.tensor_tensor(
        dsc[:, :].rearrange("q (tc h) -> q tc h", tc=32),
        dtt[:, :].rearrange("q (tc h) -> q tc h", tc=32),
        arow_bc[:, :].unsqueeze(1).broadcast_to([128, 32, 32]), op=ALU.mult),
        deps=[t_s6, t_b2e, t_s5])

    cs_all = p.sbuf("cs_all", [128, 32 * 32], F32)
    csT_sb = p.sbuf("csT_sb", [32, 128], F32)
    bcseg = [p.sbuf(f"bcseg{i}", [128, 2048], F32) for i in range(4)]
    csd_s = p.dtmp("csd_s", [32, 32, 128], F32)
    dec_bf = p.sbuf("dec_bf", [128, 32], BF16)
    dec_sc = p.sbuf("dec_sc", [128, 32], F32)
    expcs = p.sbuf("expcs", [128, 32], F32)
    xc = [p.sbuf(f"xc{i}", [128, 2048], BF16) for i in range(2)]
    xdt = p.sbuf("xdt", [128, 2048], BF16)
    xdd = p.sbuf("xdd", [128, 2048], BF16)
    scm = p.sbuf("scm", [128, 256], BF16)
    slb = [p.sbuf(f"slb{i}", [128, 2048], BF16) for i in range(4)]
    carry = p.sbuf("carry", [128, 2048], BF16)
    cdx = p.sbuf("cdx", [128, 1024], BF16)
    yo_s = p.sbuf("yo_s", [128, 1024], BF16)
    xdsc = p.sbuf("xdsc", [128, 1024], BF16)
    ystg = [p.sbuf(f"ystg{i}", [128, 1024], BF16) for i in range(2)]

    ps_y = p.psum("ps_y", [128, 1024], F32)
    ps_yo = p.psum("ps_yo", [128, 1024], F32)
    ps_st = p.psum("ps_st", [128, 1024], F32)
    ps_sc2 = p.psum("ps_sc2", [128, 512], F32)

    t_czero = p.op("gpsimd", lambda e: e.memset(carry[:, :], 0.0), deps=conv_end)

    csrow_free = [(), ()]
    dec_free = ()
    expcs_free = ()
    xc_free = [(), ()]
    xdt_free = ()
    xdd_free = [(), ()]
    slb_free = [(), (), (), ()]
    cdx_free = ()
    yos_free = ()
    ystg_free = [(), ()]
    psy_free = conv_end
    psyo_free = conv_end
    psst_free = conv_end
    pssc2_free = conv_end
    bcseg_free = [(), (), (), ()]
    xdsc_free = conv_end
    carry_w = {}

    for tc in range(32):
        rp = tc % 2
        t_cs = psm_op("tensor", (lambda tc: lambda e: e.matmul(
            ps_sm[:, 0:32], triu[:, :], dsc[:, tc * 32:(tc + 1) * 32],
            start=True, stop=True))(tc), deps=[t_dA, t_triu])
        t_cse = psm_op("vector", (lambda tc: lambda e: e.tensor_copy(
            cs_all[:, tc * 32:(tc + 1) * 32], ps_sm[:, 0:32]))(tc))
        t_csT = psm_op("tensor", (lambda tc: lambda e: e.transpose(
            ps_sm[0:32, 128:256], cs_all[:, tc * 32:(tc + 1) * 32], identf[:, :]))(tc),
            deps=[t_identf])
        t_csTe = psm_op("vector", lambda e: e.tensor_copy(csT_sb[0:32, :], ps_sm[0:32, 128:256]))
        t_csd = p.dma("sync", f"csd{rp}", (lambda tc: lambda e: e.dma_start(
            out=csd_s[tc, :, :], in_=csT_sb[0:32, :]))(tc),
            deps=[t_csTe] + list(csrow_free[rp]))
        t_ecs = p.op("scalar", (lambda tc: lambda e: e.activation(
            expcs[:, :], cs_all[:, tc * 32:(tc + 1) * 32], AF.Exp))(tc),
            deps=[t_cse] + list(expcs_free))
        t_xc = p.dma("sync", f"xc{rp}", (lambda tc, rp: lambda e: e.dma_start(
            out=xc[rp][:, :], in_=xtm_s[tc * 128:(tc + 1) * 128, :]))(tc, rp),
            deps=list(xc_free[rp]) + (list(xtm_done) if tc == 0 else []))
        t_xdt = p.op("vector", (lambda tc, rp: lambda e: e.tensor_tensor(
            xdt[:, :].rearrange("q (h w) -> q h w", h=32), xc[rp][:, :].rearrange("q (h w) -> q h w", h=32),
            dtt_bf[:, tc * 32:(tc + 1) * 32].unsqueeze(2).broadcast_to([128, 32, 64]),
            op=ALU.mult))(tc, rp), deps=[t_xc, t_s7] + list(xdt_free))
        dec_toks = []

        hg_toks = []
        t_scm_prev = ()
        for g in range(2):
            sb2 = (tc % 2) * 2 + g
            t_smm = p.op("tensor", (lambda g, tc: lambda e: e.matmul(
                ps_sc2[:, 0:128], bcf[:, g * S + tc * 128:g * S + tc * 128 + 128],
                bcf[:, (2 + g) * S + tc * 128:(2 + g) * S + tc * 128 + 128],
                start=True, stop=True))(g, tc),
                deps=list(bcf_ready) + list(t_scm_prev) + list(pssc2_free))
            t_scm = p.op("vector", (lambda g: lambda e: e.tensor_tensor(
                scm[:, g * 128:(g + 1) * 128], ps_sc2[:, 0:128], triu[:, :],
                op=ALU.mult))(g), deps=[t_triu, t_smm])
            pssc2_free = (t_scm,)
            t_scm_prev = (t_scm,)
            t_bc = p.dma("sync", f"bcs{sb2}", (lambda tc, g, sb2: lambda e: e.dma_start(
                out=bcseg[sb2][:, :],
                in_=csd_s[tc].rearrange("h l -> (h l)")[g * 2048:(g + 1) * 2048]
                    .unsqueeze(0).broadcast_to([128, 2048])))(tc, g, sb2),
                deps=[t_csd] + list(bcseg_free[sb2]))
            t_sub = p.op("vector", (lambda tc, g, sb2: lambda e: e.tensor_tensor(
                slb[sb2][:, :].rearrange("q (h l) -> q h l", h=16),
                bcseg[sb2][:, :].rearrange("q (h l) -> q h l", h=16),
                cs_all[:, tc * 32 + g * 16:tc * 32 + (g + 1) * 16].unsqueeze(2).broadcast_to([128, 16, 128]),
                op=ALU.subtract))(tc, g, sb2), deps=[t_bc, t_cse] + list(slb_free[sb2]))
            t_dsub = p.op("vector", (lambda tc, g, sb2: lambda e: e.tensor_tensor(
                dec_sc[:, g * 16:(g + 1) * 16], bcseg[sb2][:, 127::128],
                cs_all[:, tc * 32 + g * 16:tc * 32 + (g + 1) * 16],
                op=ALU.subtract))(tc, g, sb2), deps=[t_bc, t_cse] + list(dec_free))
            t_dec = p.op("scalar", (lambda g: lambda e: e.activation(
                dec_bf[:, g * 16:(g + 1) * 16], dec_sc[:, g * 16:(g + 1) * 16], AF.Exp))(g),
                deps=[t_dsub])
            dec_toks.append(t_dec)
            t_rl = p.op("scalar", (lambda sb2: lambda e: e.activation(
                slb[sb2][:, :], slb[sb2][:, :], AF.Relu, scale=-1.0))(sb2), deps=[t_sub])
            t_exp = p.op("scalar", (lambda sb2: lambda e: e.activation(
                slb[sb2][:, :], slb[sb2][:, :], AF.Exp, scale=-1.0))(sb2), deps=[t_rl])
            t_sl = p.op("vector", (lambda g, sb2: lambda e: e.tensor_tensor(
                slb[sb2][:, :].rearrange("q (h l) -> q h l", h=16),
                slb[sb2][:, :].rearrange("q (h l) -> q h l", h=16),
                scm[:, g * 128:(g + 1) * 128].unsqueeze(1).broadcast_to([128, 16, 128]),
                op=ALU.mult))(g, sb2), deps=[t_exp, t_scm])
            t_sls = [t_sl]
            yoffs = []
            for half in range(2):
                yoffs.append(p.op("tensor", (lambda g, tc, half: lambda e: e.matmul(
                    ps_yo[:, half * 512:(half + 1) * 512],
                    bcf[:, (2 + g) * S + tc * 128:(2 + g) * S + tc * 128 + 128],
                    carry[:, g * 1024 + half * 512:g * 1024 + (half + 1) * 512],
                    start=True, stop=True))(g, tc, half),
                    deps=list(psyo_free) + list(bcf_ready) + [carry_w.get(g, t_czero)]))
            ydiags = []
            for h16 in range(16):
                hl = g * 16 + h16
                ydiags.append(p.op("tensor", (lambda sb2, h16, hl: lambda e: e.matmul(
                    ps_y[:, h16 * 64:(h16 + 1) * 64],
                    slb[sb2][:, h16 * 128:(h16 + 1) * 128],
                    xdt[:, hl * 64:(hl + 1) * 64],
                    start=True, stop=True))(sb2, h16, hl),
                    deps=[t_xdt] + t_sls + list(psy_free)))
            slb_free[sb2] = (ydiags[-1],)
            t_xdd = p.op("vector", (lambda tc, g: lambda e: e.tensor_tensor(
                xdd[:, g * 1024:(g + 1) * 1024].rearrange("q (h w) -> q h w", h=16),
                xdt[:, g * 1024:(g + 1) * 1024].rearrange("q (h w) -> q h w", h=16),
                dec_bf[:, g * 16:(g + 1) * 16].unsqueeze(2).broadcast_to([128, 16, 64]),
                op=ALU.mult))(tc, g), deps=[t_xdt, t_dec] + list(xdd_free[g]))
            stts = []
            for half in range(2):
                stts.append(p.op("tensor", (lambda g, tc, half: lambda e: e.matmul(
                    ps_st[:, half * 512:(half + 1) * 512],
                    bct[:, tc * 256 + g * 128:tc * 256 + (g + 1) * 128],
                    xdd[:, g * 1024 + half * 512:g * 1024 + (half + 1) * 512],
                    start=True, stop=True))(g, tc, half),
                    deps=list(psst_free) + list(bct_ready) + [t_xdd]))
            t_cdx = p.op("scalar", (lambda sb2: lambda e: e.activation(
                cdx[:, :].rearrange("q (h w) -> q h w", h=16),
                bcseg[sb2][:, 127::128].unsqueeze(2).broadcast_to([128, 16, 64]),
                AF.Exp))(sb2), deps=[t_sub] + list(cdx_free))
            t_c1 = p.op("vector", (lambda g: lambda e: e.tensor_tensor(
                cdx[:, :], carry[:, g * 1024:(g + 1) * 1024], cdx[:, :], op=ALU.mult))(g),
                deps=[t_cdx, carry_w.get(g, t_czero), yoffs[-1]])
            t_c2 = p.op("vector", (lambda g: lambda e: e.tensor_tensor(
                carry[:, g * 1024:(g + 1) * 1024], cdx[:, :], ps_st[:, :], op=ALU.add))(g),
                deps=[t_c1] + stts)
            cdx_free = (t_c2,)
            carry_w[g] = t_c2
            psst_free = (t_c2,)
            yb = tc % 2
            t_yo = p.op("vector", (lambda g, yb: lambda e: e.tensor_tensor(
                ystg[yb][:, :].rearrange("q (h w) -> q h w", h=16),
                ps_yo[:, :].rearrange("q (h w) -> q h w", h=16),
                expcs[:, g * 16:(g + 1) * 16].unsqueeze(2).broadcast_to([128, 16, 64]),
                op=ALU.mult))(g, yb), deps=yoffs + [t_ecs] + list(ystg_free[yb]))
            expcs_free = (t_yo,)
            psyo_free = (t_yo,)
            t_y1 = p.op("vector", (lambda yb: lambda e: e.tensor_tensor(
                ystg[yb][:, :], ystg[yb][:, :], ps_y[:, :], op=ALU.add))(yb),
                deps=ydiags + [t_yo])
            psy_free = (t_y1,)
            t_y2 = p.op("vector", (lambda g, rp: lambda e: e.tensor_tensor(
                xdsc[:, :], xc[rp][:, g * 1024:(g + 1) * 1024], dxd_bc[:, g * 1024:(g + 1) * 1024],
                op=ALU.mult))(g, rp), deps=[t_xc, t_dxd] + list(xdsc_free))
            t_y3 = p.op("vector", (lambda yb: lambda e: e.tensor_tensor(
                ystg[yb][:, :], ystg[yb][:, :], xdsc[:, :], op=ALU.add))(yb), deps=[t_y2, t_y1])
            xdsc_free = (t_y3,)
            t_yout = p.dma("scalar", f"yo{yb}", (lambda tc, g, yb: lambda e: e.dma_start(
                out=y_d[tc * 128:(tc + 1) * 128, g * 1024:(g + 1) * 1024],
                in_=ystg[yb][:, :]))(tc, g, yb), deps=[t_y3])
            ystg_free[yb] = (t_yout,)
            hg_toks.append(t_y3)
            bcseg_free[sb2] = (t_sub, t_cdx)
        xdt_free = tuple(hg_toks)
        xdd_free = [tuple(hg_toks), tuple(hg_toks)]
        dec_free = tuple(dec_toks)
        xc_free[rp] = tuple(hg_toks) + (t_y2,)
        csrow_free[rp] = tuple(hg_toks)
    return p


def run_E(pj, conv_w, conv_b, dt_bias, Dv, y_full_out=None):
    NCK_ = NCK
    p = build_launch_E()
    nc = p.build()
    cw = conv_w.reshape(10240, 4)
    in_maps = []
    for c in range(NCORES):
        b = c // 4
        g0 = 2 * (c % 4)
        rows = slice(b * S, (b + 1) * S)
        xcols = slice(8192 + 1024 * g0, 8192 + 1024 * g0 + 2048)
        xsl = pj[rows, xcols]
        xhT = np.zeros((2048, S + 3), dtype=NPBF16)
        xhT[:, 3:] = xsl.T
        bcT = np.zeros((512, S + 3), dtype=NPBF16)
        for i, off in enumerate([8192 + 8192 + 128 * g0, 8192 + 8192 + 128 * (g0 + 1),
                                 8192 + 9216 + 128 * g0, 8192 + 9216 + 128 * (g0 + 1)]):
            bcT[i * 128:(i + 1) * 128, 3:] = pj[rows, off:off + 128].T
        dtc = np.ascontiguousarray(pj[rows, 18432 + 16 * g0:18432 + 16 * g0 + 32])
        ch_idx = (list(range(1024 * g0, 1024 * g0 + 2048)) +
                  list(range(8192 + 128 * g0, 8192 + 128 * g0 + 256)) +
                  list(range(9216 + 128 * g0, 9216 + 128 * g0 + 256)))
        cwc = np.ascontiguousarray(cw[ch_idx].reshape(NCK_, 128, 4).transpose(1, 0, 2).reshape(128, NCK_ * 4))
        cbc = np.ascontiguousarray(conv_b[ch_idx].reshape(NCK_, 128).T)
        h0 = 16 * g0
        in_maps.append({
            "xhT": np.ascontiguousarray(xhT), "bcT": np.ascontiguousarray(bcT),
            "dtc": dtc, "cw": cwc, "cb": cbc,
            "dtb": np.ascontiguousarray(dt_bias[h0:h0 + 32]),
            "Dh": np.ascontiguousarray(Dv[h0:h0 + 32]),
            "arow": -np.arange(h0 + 1, h0 + 33, dtype=np.float32),
        })
    res = run_bass_kernel_spmd(nc, in_maps, core_ids=list(range(NCORES)))
    y = np.zeros((T, 8192), dtype=NPBF16)
    for c in range(NCORES):
        b = c // 4
        g0 = 2 * (c % 4)
        y[b * S:(b + 1) * S, 1024 * g0:1024 * g0 + 2048] = res.results[c]["y"]
    return y



INTER = 8192


def build_launch_F():
    p = Prog("F")
    y_din = p.din("y", [TPC, INTER], BF16)
    g_din = p.din("gate", [TPC, INTER], BF16)
    x2_d = p.din("x2", [TPC, D], F32)
    nw_d = p.din("norm_w", [INTER], F32)
    wop_d = p.din("wopT", [INTER, D], F32)
    x3_d = p.dout("x3", [TPC, D], F32)
    op_s = p.dtmp("op_s", [TPC, D], F32)

    ident, ones, eps_c, t_ident, t_ones, t_eps = emit_const_setup(p)
    p.eps_c, p.t_eps = eps_c, t_eps

    nw_bf = p.sbuf("nw_bf", [1, INTER], BF16)
    nwx = p.sbuf("nwx", [128, INTER], BF16)
    ssq = p.sbuf("ssq", [128, 8], F32)
    rst = p.sbuf("rst", [128, 8], F32)
    ps_nb = p.psum("ps_nb", [128, 512], F32)
    ps_tr = [p.psum(f"ps_tr{i}", [128, 1024], BF16) for i in range(2)]
    ps_mm = [p.psum(f"ps_mm{i}", [128, 512], F32) for i in range(4)]

    t_nw = p.dma("gpsimd", "smallw", lambda e: e.dma_start(out=nw_bf[0:1, :], in_=nw_d.unsqueeze(0)))
    nb_prev = ()
    t_nbs = []
    for i in range(16):
        t_m = p.op("tensor", (lambda i: lambda e: e.matmul(
            ps_nb[:, :], ones[0:1, :], nw_bf[0:1, i * 512:(i + 1) * 512],
            start=True, stop=True))(i), deps=[t_nw, t_ones] + list(nb_prev))
        t_e = p.op("vector", (lambda i: lambda e: e.tensor_copy(
            nwx[:, i * 512:(i + 1) * 512], ps_nb[:, :]))(i), deps=[t_m])
        nb_prev = (t_e,)
        t_nbs.append(t_e)
    t_nwx = tuple(t_nbs)

    yin = [p.sbuf(f"yin{i}", [128, INTER], BF16) for i in range(2)]
    gin = [p.sbuf(f"gin{i}", [128, INTER], BF16) for i in range(2)]
    ygnT = p.sbuf("ygnT", [128, 4 * INTER], BF16)
    wt = [p.sbuf(f"wtq{i}", [128, 16 * 512], BF16) for i in range(2)]
    stg = [p.sbuf(f"stg{i}", [128, 512], F32) for i in range(2)]

    yin_free = [(), ()]
    gin_free = [(), ()]
    trps_free = [(), ()]
    ygnT_free = ()
    ssq_free = ()
    rst_free = ()
    wt_free = [(), ()]
    stg_free = [(), ()]
    bank_free = [()] * 4
    last_op = []
    it = 0
    for half in range(2):
        halfA_end = []
        for t4 in range(4):
            t = half * 4 + t4
            pb = t4 % 2
            t_y = p.dma("sync", f"yin{pb}", (lambda t, pb: lambda e: e.dma_start(
                out=yin[pb][:, :], in_=y_din[t * 128:(t + 1) * 128, :]))(t, pb),
                deps=yin_free[pb])
            t_g = p.dma("sync", f"gin{pb}", (lambda t, pb: lambda e: e.dma_start(
                out=gin[pb][:, :], in_=g_din[t * 128:(t + 1) * 128, :]))(t, pb),
                deps=gin_free[pb])
            t_sg = p.op("scalar", (lambda pb: lambda e: e.activation(
                gin[pb][:, :], gin[pb][:, :], AF.Silu))(pb), deps=[t_g])
            t_yg = p.op("vector", (lambda pb: lambda e: e.tensor_tensor(
                yin[pb][:, :], yin[pb][:, :], gin[pb][:, :], op=ALU.mult))(pb),
                deps=[t_y, t_sg])
            sq_toks = []
            for g8 in range(8):
                sq_toks.append(p.op("scalar", (lambda pb, g8: lambda e: e.activation(
                    gin[pb][:, g8 * 1024:(g8 + 1) * 1024], yin[pb][:, g8 * 1024:(g8 + 1) * 1024],
                    AF.Square, accum_out=ssq[:, g8:g8 + 1]))(pb, g8),
                    deps=[t_yg] + list(ssq_free)))
            gin_free[pb] = (sq_toks[-1],)
            t_l = p.op("scalar", lambda e: e.activation(rst[:, 0:8], ssq[:, 0:8], AF.Ln,
                                                        scale=1.0 / 1024.0, bias=eps_c[:, 0:1]),
                       deps=sq_toks + [t_eps] + list(rst_free))
            t_rs = p.op("scalar", lambda e: e.activation(rst[:, 0:8], rst[:, 0:8], AF.Exp, scale=-0.5),
                        deps=[t_l])
            ssq_free = (t_l,)
            ap_toks = []
            for g8 in range(8):
                ap_toks.append(p.op("vector", (lambda pb, g8: lambda e: e.tensor_scalar_mul(
                    yin[pb][:, g8 * 1024:(g8 + 1) * 1024], yin[pb][:, g8 * 1024:(g8 + 1) * 1024],
                    rst[:, g8:g8 + 1]))(pb, g8), deps=[t_rs]))
            t_n = p.op("vector", (lambda pb: lambda e: e.tensor_tensor(
                yin[pb][:, :], yin[pb][:, :], nwx[:, :], op=ALU.mult))(pb),
                deps=ap_toks + list(t_nwx))
            rst_free = tuple(ap_toks)
            ev_toks = []
            for grp in range(8):
                tp = grp % 2
                trs = []
                for j in range(8):
                    ic = grp * 8 + j
                    trs.append(p.op("tensor", (lambda pb, tp, j, ic: lambda e: e.transpose(
                        ps_tr[tp][:, j * 128:(j + 1) * 128],
                        yin[pb][:, ic * 128:(ic + 1) * 128], ident[:, :]))(pb, tp, j, ic),
                        deps=[t_n, t_ident] + list(trps_free[tp])))
                t_ev = p.op("vector", (lambda t4, grp, tp: lambda e: e.tensor_copy(
                    ygnT[:, t4 * INTER + grp * 1024:t4 * INTER + (grp + 1) * 1024],
                    ps_tr[tp][:, :]))(t4, grp, tp), deps=trs + list(ygnT_free))
                trps_free[tp] = (t_ev,)
                ev_toks.append(t_ev)
            yin_free[pb] = (ev_toks[-1],)
            halfA_end.extend(ev_toks)
        halfA_end = tuple(halfA_end)
        for ob in range(8):
            last_mms = [None] * 4
            for kh in range(4):
                wb = it % 2
                it += 1
                wtoks = []
                for part in range(2):
                    kc0, kc1 = part * 8, (part + 1) * 8
                    wtoks.append(p.dma("gpsimd", f"wload{wb}", (lambda ob, kh, wb, kc0, kc1: lambda e: e.dma_start(
                        out=wt[wb].rearrange("p (kc o) -> p kc o", kc=16)[:, kc0:kc1, :],
                        in_=wop_d.rearrange("(kh kc p) o -> kh p kc o", kh=4, p=128)[kh, :, kc0:kc1, ob * 512:(ob + 1) * 512]))(ob, kh, wb, kc0, kc1),
                        deps=wt_free[wb]))
                for t4 in range(4):
                    for kc in range(16):
                        dep0 = (list(wtoks) + list(bank_free[t4]) + list(halfA_end)) if kc == 0 else []
                        last_mms[t4] = p.op("tensor", (lambda t4, kh, kc, wb: lambda e: e.matmul(
                            ps_mm[t4][:, :],
                            ygnT[:, t4 * INTER + (kh * 16 + kc) * 128:t4 * INTER + (kh * 16 + kc + 1) * 128],
                            wt[wb][:, kc * 512:(kc + 1) * 512],
                            start=(kh == 0 and kc == 0), stop=(kh == 3 and kc == 15)))(t4, kh, kc, wb),
                            deps=dep0)
                wt_free[wb] = (last_mms[3],)
            for t4 in range(4):
                sg = t4 % 2
                t_evac = p.op("vector", (lambda sg, t4: lambda e: e.tensor_copy(
                    stg[sg][:, :], ps_mm[t4][:, :]))(sg, t4), deps=[last_mms[t4]] + list(stg_free[sg]))
                bank_free[t4] = (t_evac,)
                t_st = p.dma("scalar", f"oout{sg}", (lambda half, ob, t4, sg: lambda e: e.dma_start(
                    out=op_s[(half * 4 + t4) * 128:(half * 4 + t4 + 1) * 128, ob * 512:(ob + 1) * 512],
                    in_=stg[sg][:, :]))(half, ob, t4, sg), deps=[t_evac])
                stg_free[sg] = (t_st,)
        ygnT_free = (last_mms[3],)
        last_op.append(("D:oout0", p.dma_cnt.get("oout0", 0)))
        last_op.append(("D:oout1", p.dma_cnt.get("oout1", 0)))

    p.sfree("stg1", "stg0", "wtq1", "wtq0", "ygnT", "gin1", "gin0", "yin1", "yin0")
    stat = [[p.sbuf(f"st{n}{i}", [128, 1], F32) for n in ("m", "v", "r", "n")] for i in range(2)]
    fin = [p.sbuf(f"fin{i}", [128, D], F32) for i in range(2)]
    x2b = [p.sbuf(f"x2b{i}", [128, D], F32) for i in range(2)]
    fsq = p.sbuf("fsq", [128, D], BF16)
    fo_done = tuple(last_op[-2:])
    fin_free = [(), ()]
    x2b_free = [(), ()]
    fsq_free = ()
    cstat_free = [(), ()]
    for t in range(NTT):
        pb = t % 2
        t_f = p.dma("sync", f"fin{pb}", (lambda t, pb: lambda e: e.dma_start(
            out=fin[pb][:, :], in_=op_s[t * 128:(t + 1) * 128, :]))(t, pb),
            deps=list(fo_done) + list(fin_free[pb]))
        t_x2l = p.dma("sync", f"x2in{pb}", (lambda t, pb: lambda e: e.dma_start(
            out=x2b[pb][:, :], in_=x2_d[t * 128:(t + 1) * 128, :]))(t, pb),
            deps=x2b_free[pb])
        m_, v_, r_, n_ = stat[pb]
        t_r, t_n, t_sq = emit_ln_stats(p, fin[pb][:, :], fsq[:, :], m_, v_, r_, n_, D,
                                       deps=[t_f] + list(cstat_free[pb]) + list(fsq_free))
        fsq_free = (t_sq,)
        t_fn = p.op("scalar", (lambda pb, r_, n_: lambda e: e.activation(
            fin[pb][:, :], fin[pb][:, :], AF.Identity, bias=n_[:, 0:1], scale=r_[:, 0:1]))(pb, r_, n_),
            deps=[t_r, t_n, t_sq])
        t_x3 = p.op("vector", (lambda pb: lambda e: e.tensor_tensor(
            x2b[pb][:, :], x2b[pb][:, :], fin[pb][:, :], op=ALU.add))(pb),
            deps=[t_fn, t_x2l])
        cstat_free[pb] = (t_fn, t_n)
        fin_free[pb] = (t_x3,)
        t_o = p.dma("scalar", f"x3o{pb}", (lambda t, pb: lambda e: e.dma_start(
            out=x3_d[t * 128:(t + 1) * 128, :], in_=x2b[pb][:, :]))(t, pb), deps=[t_x3])
        x2b_free[pb] = (t_o,)
    return p


def run_F(y, gate, x2, norm_w, out_proj_w):
    p = build_launch_F()
    nc = p.build()
    wopT = np.ascontiguousarray(out_proj_w.T)
    in_maps = []
    for c in range(NCORES):
        sl = slice(c * TPC, (c + 1) * TPC)
        in_maps.append({"y": np.ascontiguousarray(y[sl]), "gate": np.ascontiguousarray(gate[sl]),
                        "x2": np.ascontiguousarray(x2[sl]), "norm_w": norm_w, "wopT": wopT})
    res = run_bass_kernel_spmd(nc, in_maps, core_ids=list(range(NCORES)))
    return np.concatenate([r["x3"] for r in res.results], axis=0)



def build_launch_G():
    p = Prog("G")
    xP_d = p.din("xP", [TPC, D], F32)
    up_d = p.din("up", [16, 256, 1024], F32)
    dn_d = p.din("down", [16, 1024, 256], F32)
    out_d = p.dout("x4P", [TPC, D], F32)

    ident, ones, eps_c, t_ident, t_ones, t_eps = emit_const_setup(p)
    p.eps_c, p.t_eps = eps_c, t_eps

    zPT = p.sbuf("zPT", [128, NTT * D], BF16)
    yP = p.sbuf("yP", [128, NTT * D], BF16)
    stat = [[p.sbuf(f"st{n}{i}", [128, 1], F32) for n in ("m", "v", "r", "n")] for i in range(2)]
    ps_tr = [p.psum(f"ps_tr{i}", [128, 1024], BF16) for i in range(2)]
    xin = [p.sbuf(f"xin{i}", [128, D], F32) for i in range(2)]
    z = [p.sbuf(f"z{i}", [128, D], BF16) for i in range(2)]

    xin_free = [(), ()]
    stat_free = [(), ()]
    z_free = [(), ()]
    trps_free = [(), ()]
    phaseA_end = []
    for t in range(NTT):
        pb = t % 2
        t_x = p.dma("sync", f"xin{pb}", (lambda t, pb: lambda e: e.dma_start(
            out=xin[pb][:, :], in_=xP_d[t * 128:(t + 1) * 128, :]))(t, pb),
            deps=xin_free[pb])
        m_, v_, r_, n_ = stat[pb]
        t_r, t_n, t_sq = emit_ln_stats(p, xin[pb][:, :], z[pb][:, :], m_, v_, r_, n_, D,
                                       deps=[t_x] + list(stat_free[pb]) + list(z_free[pb]))
        t_z = p.op("scalar", (lambda pb, r_, n_: lambda e: e.activation(
            z[pb][:, :], xin[pb][:, :], AF.Identity, bias=n_[:, 0:1], scale=r_[:, 0:1]))(pb, r_, n_),
            deps=[t_r, t_n, t_sq])
        stat_free[pb] = (t_z, t_n)
        xin_free[pb] = (t_z,)
        evac_toks = []
        for grp in range(4):
            tp = grp % 2
            tr_toks = []
            for j in range(8):
                kc = grp * 8 + j
                tr_toks.append(p.op("tensor", (lambda pb, tp, j, kc: lambda e: e.transpose(
                    ps_tr[tp][:, j * 128:(j + 1) * 128],
                    z[pb][:, kc * 128:(kc + 1) * 128], ident[:, :]))(pb, tp, j, kc),
                    deps=[t_z, t_ident] + list(trps_free[tp])))
            t_evac = p.op("vector", (lambda t, grp, tp: lambda e: e.tensor_copy(
                zPT[:, t * D + grp * 1024: t * D + (grp + 1) * 1024], ps_tr[tp][:, :]))(t, grp, tp),
                deps=tr_toks)
            trps_free[tp] = (t_evac,)
            evac_toks.append(t_evac)
        z_free[pb] = (evac_toks[-1],)
        phaseA_end.extend(evac_toks)
    phaseA_end = tuple(phaseA_end) + xin_free[0] + xin_free[1]
    p.sfree("z1", "z0")

    p.sfree("ps_tr1", "ps_tr0")
    up_sb = [p.sbuf(f"up{i}", [128, 2 * 1024], BF16) for i in range(2)]
    dn_sb = [p.sbuf(f"dn{i}", [128, 8 * 256], BF16) for i in range(2)]
    gT = p.sbuf("gT", [128, 8 * 1024], BF16)
    yT = p.sbuf("yT", [128, 2 * 1024], BF16)
    ps_g = p.psum("ps_g", [128, 1024], F32)
    ps_d = p.psum("ps_d", [128, 2048], F32)
    ps_t2 = p.psum("ps_t2", [128, 256], BF16)

    up_free = [phaseA_end, phaseA_end]
    dn_free = [phaseA_end, phaseA_end]
    gT_free = phaseA_end
    yT_free = phaseA_end
    psg_free = phaseA_end
    psd_free = phaseA_end
    pst2_free = phaseA_end
    yP_w = []
    for h in range(16):
        hb = h % 2
        t_up = p.dma("gpsimd", f"upl{hb}", (lambda h, hb: lambda e: e.dma_start(
            out=up_sb[hb][:, :].rearrange("p (kc m) -> p kc m", kc=2),
            in_=up_d[h].rearrange("(kc p) m -> p kc m", p=128)))(h, hb), deps=up_free[hb])
        t_dn = p.dma("gpsimd", f"dnl{hb}", (lambda h, hb: lambda e: e.dma_start(
            out=dn_sb[hb][:, :].rearrange("p (mc d) -> p mc d", mc=8),
            in_=dn_d[h].rearrange("(mc p) d -> p mc d", p=128)))(h, hb), deps=dn_free[hb])
        up_reads = []
        dn_reads = []
        gel_toks = []
        for mc in range(8):
            gmms = []
            for nh in range(2):
                for kc in range(2):
                    gmms.append(p.op("tensor", (lambda h, hb, mc, nh, kc: lambda e: e.matmul(
                        ps_g[:, nh * 512:(nh + 1) * 512],
                        up_sb[hb][:, kc * 1024 + mc * 128:kc * 1024 + (mc + 1) * 128],
                        zPT[:, :].rearrange("q (t kk c) -> q kk t c", t=NTT, kk=32)
                            [:, h * 2 + kc, nh * 4:(nh + 1) * 4, :],
                        start=(kc == 0), stop=(kc == 1)))(h, hb, mc, nh, kc),
                        deps=([t_up] + list(psg_free)) if (nh == 0 and kc == 0) else []))
            t_gelu = p.op("scalar", (lambda mc: lambda e: e.activation(
                gT[:, mc * 1024:(mc + 1) * 1024], ps_g[:, :], AF.Gelu))(mc),
                deps=gmms + list(gT_free))
            gel_toks.append(t_gelu)
            psg_free = (t_gelu,)
        up_reads.append(gmms[-1])
        dmms = []
        for dc in range(2):
            for nh in range(2):
                for mc in range(8):
                    dmms.append(p.op("tensor", (lambda hb, dc, nh, mc: lambda e: e.matmul(
                        ps_d[:, dc * 1024 + nh * 512:dc * 1024 + (nh + 1) * 512],
                        dn_sb[hb][:, mc * 256 + dc * 128:mc * 256 + (dc + 1) * 128],
                        gT[:, mc * 1024 + nh * 512:mc * 1024 + (nh + 1) * 512],
                        start=(mc == 0), stop=(mc == 7)))(hb, dc, nh, mc),
                        deps=([t_dn] + gel_toks + list(psd_free)) if (dc == 0 and nh == 0 and mc == 0) else []))
        dn_reads.append(dmms[-1])
        gT_free = (dmms[-1],)
        t_ye = p.op("vector", lambda e: e.tensor_copy(yT[:, :], ps_d[:, :]),
                    deps=[dmms[-1]] + list(yT_free))
        psd_free = (t_ye,)
        tr2 = []
        ev2 = []
        for t in range(NTT):
            for dc in range(2):
                tr2.append(p.op("tensor", (lambda t, dc: lambda e: e.transpose(
                    ps_t2[:, dc * 128:(dc + 1) * 128],
                    yT[:, dc * 1024 + t * 128:dc * 1024 + (t + 1) * 128],
                    ident[:, :]))(t, dc), deps=[t_ye, t_ident] + list(pst2_free)))
            t_yw = p.op("vector", (lambda t, h: lambda e: e.tensor_copy(
                yP[:, t * D + h * 256:t * D + (h + 1) * 256], ps_t2[:, :]))(t, h),
                deps=tr2[-2:])
            pst2_free = (t_yw,)
            ev2.append(t_yw)
        yT_free = (tr2[-1],)
        yP_w.extend(ev2)
        up_free[hb] = (up_reads[-1],)
        dn_free[hb] = (dn_reads[-1],)
        yP_done = tuple(yP_w[-NTT:]) + tuple(yP_w[:1])

    p.sfree("yT", "gT", "dn1", "dn0", "up1", "up0")
    fin = p.sbuf("fin", [128, D], F32)
    fin_free = ()
    cstat_free = [stat_free[0], stat_free[1]]
    xin2_free = [xin_free[0], xin_free[1]]
    for t in range(NTT):
        pb = t % 2
        t_x = p.dma("sync", f"xrin{pb}", (lambda t, pb: lambda e: e.dma_start(
            out=xin[pb][:, :], in_=xP_d[t * 128:(t + 1) * 128, :]))(t, pb),
            deps=xin2_free[pb])
        m_, v_, r_, n_ = stat[pb]
        t_r, t_n, t_sq = emit_ln_stats(p, yP[:, t * D:(t + 1) * D], fin[:, :], m_, v_, r_, n_, D,
                                       deps=list(yP_done) + list(cstat_free[pb]) + list(fin_free))
        t_fn = p.op("scalar", (lambda t, r_, n_: lambda e: e.activation(
            fin[:, :], yP[:, t * D:(t + 1) * D], AF.Identity, bias=n_[:, 0:1],
            scale=r_[:, 0:1]))(t, r_, n_), deps=[t_r, t_n, t_sq])
        t_x4 = p.op("vector", (lambda pb: lambda e: e.tensor_tensor(
            xin[pb][:, :], xin[pb][:, :], fin[:, :], op=ALU.add))(pb),
            deps=[t_fn, t_x])
        cstat_free[pb] = (t_fn, t_n)
        fin_free = (t_x4,)
        t_o = p.dma("scalar", f"x4o{pb}", (lambda t, pb: lambda e: e.dma_start(
            out=out_d[t * 128:(t + 1) * 128, :], in_=xin[pb][:, :]))(t, pb), deps=[t_x4])
        xin2_free[pb] = (t_o, t_x4)
    return p


def run_G(xP, up, down):
    p = build_launch_G()
    nc = p.build()
    in_maps = []
    for c in range(NCORES):
        sl = slice(c * TPC, (c + 1) * TPC)
        in_maps.append({"xP": np.ascontiguousarray(xP[sl]), "up": up, "down": down})
    res = run_bass_kernel_spmd(nc, in_maps, core_ids=list(range(NCORES)))
    return np.concatenate([r["x4P"] for r in res.results], axis=0)


def run_A(x_tok, wq, bq, wk, bk, wv, bv):
    p = build_launch_A()
    nc = p.build()
    wqT = np.ascontiguousarray(wq.T)
    wkT = np.ascontiguousarray(wk.T)
    wvT = np.ascontiguousarray(wv.T)
    in_maps = []
    for c in range(NCORES):
        sl = slice(c * TPC, (c + 1) * TPC)
        in_maps.append({
            "x": np.ascontiguousarray(x_tok[sl]),
            "wqT": wqT, "wkT": wkT, "wvT": wvT,
            "bq": bq, "bk": bk, "bv": bv,
        })
    res = run_bass_kernel_spmd(nc, in_maps, core_ids=list(range(NCORES)))
    q = np.concatenate([r["q"] for r in res.results], axis=0)
    k = np.concatenate([r["k"] for r in res.results], axis=0)
    v = np.concatenate([r["v"] for r in res.results], axis=0)
    return q, k, v


def kernel(**inputs):
    _ = inputs["L0_attn_wl"], inputs["L0_attn_bl"]
    f32 = lambda k: np.ascontiguousarray(np.asarray(inputs[k], dtype=np.float32))
    x = f32("L0_attn_hidden_states").reshape(T, D)
    q, k, v = run_A(x, f32("L0_attn_wq"), f32("L0_attn_bq"), f32("L0_attn_wk"),
                    f32("L0_attn_bk"), f32("L0_attn_wv"), f32("L0_attn_bv"))
    ctx = run_B(q, k, v)
    x2 = run_C(x, ctx, f32("L0_ffn_base_W"), f32("L0_ffn_base_b"),
               f32("L0_ffn_router_W"), f32("L0_ffn_router_b"),
               f32("L0_ffn_A"), f32("L0_ffn_B"),
               np.asarray(inputs["L0_ffn_expert_map"], dtype=np.int32))
    pj = run_D(x2, f32("L1_attn_in_proj_w"))
    y = run_E(pj, f32("L1_attn_conv_w"), f32("L1_attn_conv_b"),
              f32("L1_attn_dt_bias"), f32("L1_attn_D"))
    gate = np.ascontiguousarray(pj[:, :INTER])
    x3 = run_F(y, gate, x2, f32("L1_attn_norm_w"), f32("L1_attn_out_proj_w"))
    perm = np.asarray(inputs["L1_ffn_perm"], dtype=np.int64)
    xP = np.ascontiguousarray(x3[:, perm])
    x4P = run_G(xP, f32("L1_ffn_up"), f32("L1_ffn_down"))
    out = np.empty_like(x3)
    out[:, perm] = x4P
    return out.reshape(B, S, D)
